# version 15
# baseline (speedup 1.0000x reference)
import numpy as np
import ml_dtypes
import concourse.bass as bass
import concourse.mybir as mybir
from concourse.bass_utils import run_bass_kernel_spmd

F32 = mybir.dt.float32
BF16 = mybir.dt.bfloat16
AF = mybir.ActivationFunctionType
ALU = mybir.AluOpType
AX = mybir.AxisListType

D = 1024
S = 4096
B = 4
H = 16
DH = 64
HID = 2816
E = 8
TOK = 2048
NSLOT = 4
EXT = [8, 16, 24, 32]
HC = HID // 128
HGRP = [(0, 8), (8, 16), (16, HC)]
NT = TOK // 128
DBG_CONST_BOUND = False
ENGS = ("sync", "gpsimd", "vector", "scalar", "tensor")


class Sem:
    def __init__(self, nc, name):
        self.h = nc.alloc_semaphore(name=name)
        self.v = 0
        self.name = name

    def inc(self, n=1):
        self.v += n
        return self.v


class Plan:

    def __init__(self, nc):
        self.nc = nc
        self.ops = {e: [] for e in ENGS}
        self.last_wait = {}

    def emit(self, eng, fn, sem=None, n=1):
        if sem is None:
            self.ops[eng].append(fn)
            return None
        v = sem.inc(n)
        self.ops[eng].append(lambda e, fn=fn, s=sem.h, n=n: fn(e).then_inc(s, n))
        return v

    def wait(self, eng, sem, val):
        if val is None or val <= 0:
            return
        assert val <= sem.v, f"wait on future: {eng} waits {sem.name}>={val} but planned only {sem.v}"
        key = (eng, sem.name)
        if self.last_wait.get(key, -1) >= val:
            return
        self.last_wait[key] = val
        self.ops[eng].append(lambda e, s=sem.h, v=val: e.wait_ge(s, v))


def build_nc(debug=False):
    nc = bass.Bass("TRN2", target_bir_lowering=False)

    xT = nc.dram_tensor("xT", [D, S], F32, kind="ExternalInput").ap()
    xq = nc.dram_tensor("xq", [D, TOK], F32, kind="ExternalInput").ap()
    xo = nc.dram_tensor("xo", [TOK, D], F32, kind="ExternalInput").ap()
    wqT = nc.dram_tensor("wqT", [D, D], F32, kind="ExternalInput").ap()
    wkT = nc.dram_tensor("wkT", [D, D], F32, kind="ExternalInput").ap()
    wvT = nc.dram_tensor("wvT", [D, D], F32, kind="ExternalInput").ap()
    woT = nc.dram_tensor("woT", [D, D], F32, kind="ExternalInput").ap()
    rt_raw = nc.dram_tensor("rt_raw", [E, D], F32, kind="ExternalInput").ap()
    nw = nc.dram_tensor("nw", [1, D], F32, kind="ExternalInput").ap()
    alin = nc.dram_tensor("alin", [1, 1], F32, kind="ExternalInput").ap()
    wguT = nc.dram_tensor("wguT", [E, D, 2 * HID], F32, kind="ExternalInput").ap()
    wdT = nc.dram_tensor("wdT", [E, HID, D], F32, kind="ExternalInput").ap()
    masks = nc.dram_tensor("masks", [8 * 128, 512], F32, kind="ExternalInput").ap()
    iota = nc.dram_tensor("iota", [128, E], F32, kind="ExternalInput").ap()
    xqst = nc.dram_tensor("xqst", [D, TOK], BF16, kind="Internal").ap()
    out = nc.dram_tensor("out", [TOK, D], F32, kind="ExternalOutput").ap()
    if debug:
        dbg_y = nc.dram_tensor("dbg_y", [128, 8 * TOK], F32, kind="ExternalOutput").ap()
        dbg_acc = nc.dram_tensor("dbg_acc", [128, NT * D], F32, kind="ExternalOutput").ap()
        dbg_nrm = nc.dram_tensor("dbg_nrm", [128, 8 * TOK], F32, kind="ExternalOutput").ap()
        dbg_cmb = nc.dram_tensor("dbg_cmb", [128, NT * E], F32, kind="ExternalOutput").ap()
        dbg_bnd = nc.dram_tensor("dbg_bnd", [16, TOK], F32, kind="ExternalOutput").ap()
        dbg_l = nc.dram_tensor("dbg_l", [16, TOK], F32, kind="ExternalOutput").ap()

    A0 = (nc.sbuf_base + 63) // 64 * 64
    TOPB = nc.sbuf_top
    cursor = [A0]
    used = []

    def place(name, shape, dt, region):
        nbytes = int(np.prod(shape[1:])) * (2 if dt == BF16 else 4)
        off = (cursor[0] + 63) // 64 * 64
        cursor[0] = off + nbytes
        assert cursor[0] <= TOPB, f"SBUF overflow at {name}: {cursor[0]} > {TOPB}"
        used.append((name, off, nbytes, region))
        return nc.alloc_sbuf_tensor_at(name, shape, dt, offset=off).ap()

    def place_at(name, shape, dt, off):
        nbytes = int(np.prod(shape[1:])) * (2 if dt == BF16 else 4)
        assert off + nbytes <= TOPB, f"SBUF overflow at {name}"
        return nc.alloc_sbuf_tensor_at(name, shape, dt, offset=off).ap()

    smalls = place("smalls", [128, 32], F32, "G")
    cols = place("cols", [128, 32], F32, "G")
    onesrow = place("onesrow", [1, 128], F32, "G")
    onescol = place("onescol", [128, 1], BF16, "G")
    scr = place("scr", [128, 512], BF16, "G")
    comb = place("comb", [128, NT * E], F32, "G")
    r8 = place("r8", [128, E], F32, "G")
    u8 = place("u8", [128, 8 * E], F32, "G")
    lgs = place("lgs", [128, NT * E], F32, "G")
    r2s = place("r2s", [128, NT], F32, "G")
    rtrow = place("rtrow", [1, D], F32, "G")
    nwbc = place("nwbc", [128, D], F32, "G")

    oY = cursor[0]
    y_sb = place("y_sb", [128, 8 * TOK], BF16, "Y")
    hid0 = place_at("hid0", [128, 8 * TOK], BF16, oY)

    oR2 = cursor[0]
    xTt = place("xTt", [128, 8 * S], BF16, "R2")
    acc = place_at("acc", [128, NT * D], F32, oR2)

    oR3 = cursor[0]
    vq = place("vq", [128, 32 * 4 * 65], BF16, "R3")
    kaugA = place("kaugA", [65, S], BF16, "R3")
    kaugB = place("kaugB", [65, S], BF16, "R3")
    normed = place_at("normed", [128, 8 * TOK], BF16, oR3)
    cursor[0] = max(cursor[0], oR3 + 8 * TOK * 2)

    oR4 = cursor[0]
    qaugA = place("qaugA", [65, TOK], BF16, "R4")
    qaugB = place("qaugB", [65, TOK], BF16, "R4")
    pt0 = place("pt0", [128, 512], BF16, "R4")
    pt1 = place("pt1", [128, 512], BF16, "R4")
    qsqw = place("qsqw", [128, 512], BF16, "R4")
    ksqw = place("ksqw", [128, 512], BF16, "R4")
    scrf = place("scrf", [128, 512], F32, "R4")
    wqk = [place(f"wqk{i}", [128, 2 * 8 * 128], BF16, "R4") for i in range(2)]
    wvt = [place(f"wvt{i}", [128, 8 * 256], BF16, "R4") for i in range(2)]
    masksb = place("masksb", [128, 8 * 512], BF16, "R4")
    xqt = [place(f"xqt{i}", [128, TOK], BF16, "R4") for i in range(2)]
    qn2w = place("qn2w", [1, 512], F32, "R4")
    kmaxp = place("kmaxp", [1, 16], F32, "R4")
    lnl = place("lnl", [1, 512], F32, "R4")
    rlr = place("rlr", [1, 512], F32, "R4")
    endA = cursor[0]
    cursor[0] = oR4
    wot = place("wotB", [128, 8 * D], BF16, "R4B")
    xot = [place(f"xot{i}", [128, D], F32, "R4B") for i in range(3)]
    ntok = [place(f"ntok{i}", [128, D], BF16, "R4B") for i in range(2)]

    nwsb = place("nwsb", [1, D], F32, "R4B")
    alsb = place("alsb", [1, 1], F32, "R4B")
    endB = cursor[0]
    cursor[0] = oR4
    hid1 = place("hid1", [128, 8 * TOK], BF16, "R4D")
    wg = [place(f"wg{i}", [128, 8 * 128], BF16, "R4D") for i in range(4)]
    outsb = [place(f"outsb{i}", [128, D], F32, "R4D") for i in range(2)]
    sil = [place(f"sil{i}", [128, 512], F32, "R4D") for i in range(2)]
    o_out0 = [u for u in used if u[0] == "outsb0"][0][1]
    o_out1 = [u for u in used if u[0] == "outsb1"][0][1]
    rtnw = place_at("rtnw", [128, D], F32, o_out0)
    dumpf = place_at("dumpf", [128, D], F32, o_out1)
    endD = cursor[0]
    cursor[0] = max(endA, endB, endD)

    wd = place("wd", [128, 8 * D], BF16, "R5")
    wg0 = [wd[:, (4 + i) * D:(5 + i) * D] for i in range(4)]

    assert cursor[0] <= TOPB, f"total SBUF {cursor[0] - A0} over budget"

    pb = [nc.alloc_psum_tensor(f"pb{i}", [128, 512], F32).ap() for i in range(7)]
    prow = nc.alloc_psum_tensor("prow", [1, 512], F32).ap()

    GP = [Sem(nc, f"GP{i}") for i in range(20)]
    SY = [Sem(nc, f"SY{i}") for i in range(8)]
    LXT = Sem(nc, "LXT")
    LMK = Sem(nc, "LMK")
    TS = [Sem(nc, f"TS{i}") for i in range(2)]
    STS = [Sem(nc, f"STS{i}") for i in range(2)]
    gp_i = [0]
    sy_i = [0]
    PE = Sem(nc, "PE")
    SQ = Sem(nc, "SQ")
    VE = Sem(nc, "VE")
    PH = Sem(nc, "PH")

    P = Plan(nc)
    W = P.wait
    OP = P.emit

    def WT(eng, tok):
        if tok is None:
            return
        if isinstance(tok, list):
            for t in tok:
                WT(eng, t)
            return
        P.wait(eng, tok[0], tok[1])

    def dma_pool(eng, pool, idx, o, i, **kw):
        s = pool[idx[0] % len(pool)]
        idx[0] += 1
        P.wait(eng, s, s.v)
        v = s.inc(16)
        P.ops[eng].append(lambda e, o=o, i=i, s=s.h, kw=kw: e.dma_start(out=o, in_=i, **kw).then_inc(s, 16))
        return (s, v)

    def dma_g(o, i, **kw):
        return dma_pool("gpsimd", GP, gp_i, o, i, **kw)

    def dma_s(o, i, **kw):
        return dma_pool("sync", SY, sy_i, o, i, **kw)

    def dma_fixed(eng, s, o, i, **kw):
        v = s.inc(16)
        P.ops[eng].append(lambda e, o=o, i=i, s=s.h, kw=kw: e.dma_start(out=o, in_=i, **kw).then_inc(s, 16))
        return (s, v)

    def queue_drain(eng):
        pools = {"gpsimd": GP + [LXT, LMK], "sync": SY + TS + STS}[eng]
        for s in pools:
            if s.v:
                P.wait(eng, s, s.v)

    eps5 = smalls[:, 0:1]
    eps6 = smalls[:, 1:2]

    for dc in range(8):
        xTt_tok = dma_fixed("gpsimd", LXT, xTt[:, dc * S:(dc + 1) * S], xT[dc * 128:(dc + 1) * 128, :])
    def load_wqk(buf, j):
        t1 = dma_g(wqk[buf][:, 0:1024],
                   wqT.rearrange("(dc p) n -> p dc n", p=128)[:, :, j * 128:(j + 1) * 128])
        t2 = dma_g(wqk[buf][:, 1024:2048],
                   wkT.rearrange("(dc p) n -> p dc n", p=128)[:, :, j * 128:(j + 1) * 128])
        return [t1, t2]

    def load_wv(buf, q):
        return dma_g(wvt[buf][:, :],
                     wvT.rearrange("(dc p) n -> p dc n", p=128)[:, :, q * 256:(q + 1) * 256])

    xq_stage_t = []

    def stage_xq():
        for dc in range(8):
            xq_stage_t.append(dma_g(xqst[dc * 128:(dc + 1) * 128, :], xq[dc * 128:(dc + 1) * 128, :]))

    def load_xq(buf, dc):
        WT("sync", xq_stage_t[dc])
        return dma_s(xqt[buf][:, :], xqst[dc * 128:(dc + 1) * 128, :])

    stage_xq()
    wqk_t = [load_wqk(0, 0), None]
    wv_t = [load_wv(0, 0), None]
    masks_t = dma_fixed("gpsimd", LMK, masksb[:, :], masks.rearrange("(c p) n -> p c n", p=128))

    OP("vector", lambda e: e.memset(onesrow[:, :], 1.0))
    OP("vector", lambda e: e.memset(onescol[:, :], 1.0))
    OP("vector", lambda e: e.memset(smalls[:, 0:1], 1e-5))
    OP("vector", lambda e: e.memset(smalls[:, 1:2], 1e-6))
    OP("vector", lambda e: e.memset(smalls[:, 7:8], 2.9957323))
    OP("vector", lambda e: e.memset(smalls[:, 8:9], 2.5e-6))
    OP("vector", lambda e: e.memset(kaugA[64:65, :], -1.0))
    OP("vector", lambda e: e.memset(kaugB[64:65, :], -1.0))
    vqv = vq.rearrange("p (c h q) -> p c h q", c=32, h=4)
    OP("vector", lambda e: e.memset(vqv[:, :, :, 64:65], 1.0))
    consts_t = OP("vector", lambda e: e.engine_nop(), VE)

    pt_t = [None, None]
    pt_rd = [0, 0]
    st_rd = [0, 0]
    st_wr = [0, 0]
    yev = [0, 0]
    ywr = [0, 0]
    pj_ev = [0, 0, 0, 0]
    pj_wr = [0, 0, 0, 0]
    PJB = [4, 5, 0, 1]
    prowS = [0]
    prow_sq = [0]
    ksq_rd = [0]
    qsq_rd = [0]
    lnl_rd = [0]
    rlr_rd = [0]
    prow2S = [0]
    prow2_sq = [0]
    qn2_ad = [0]
    bound_w = [[0] * 4, [0] * 4]
    rlS = [0]
    kq_ready = [0, 0]
    bound_ready = [0, 0]
    vq_ready = 0
    vq_rd = 0
    xq_t = [None, None]
    xq_rd = [0, 0]
    wqk_rd = [0, 0]
    wv_rd = [0, 0]
    qaug_rd = [0, 0]
    kaug_rd = [0, 0]

    def proj_pair(j):
        buf = j % 2
        hA, hB = 2 * j, 2 * j + 1
        qa = [qaugA, qaugB]
        ka = [kaugA, kaugB]
        WT("tensor", wqk_t[buf])
        W("tensor", LXT, LXT.v)
        kaug_free = max(kaug_rd[0], kaug_rd[1], qaug_rd[0], qaug_rd[1])
        ev_k = [None] * 8
        pend_rows = None

        def emit_rows_k(w):
            for hh in range(2):
                W("scalar", VE, ev_k[w])
                W("scalar", PE, ksq_rd[0])
                sq = OP("scalar", lambda e, w=w, hh=hh:
                        e.activation(ksqw[0:64, :], (kaugA if hh == 0 else kaugB)[0:64, w * 512:(w + 1) * 512], AF.Square), SQ)
                W("tensor", SQ, sq)
                W("tensor", VE, prowS[0])
                W("tensor", SQ, prow_sq[0])
                rw = OP("tensor", lambda e: e.matmul(prow[0:1, :], onescol[0:64, 0:1], ksqw[0:64, :], start=True, stop=True), PE)
                ksq_rd[0] = rw
                prow_sq[0] = prow_sq[0]
                W("vector", PE, rw)
                prowS[0] = OP("vector", lambda e, w=w, hh=hh:
                              e.reduce_max(kmaxp[0:1, hh * 8 + w:hh * 8 + w + 1], prow[0:1, :], axis=AX.X), VE)

        for w in range(8):
            bk = PJB[w % 4]
            W("tensor", VE, pj_ev[w % 4])
            if w % 4 >= 2:
                W("tensor", SQ, st_rd[w % 4 - 2])
            for dc in range(8):
                lhs = wqk[buf][:, 1024 + dc * 128:1024 + (dc + 1) * 128]
                rhs = xTt[:, dc * S + w * 512: dc * S + (w + 1) * 512]
                OP("tensor", lambda e, o=pb[bk], l=lhs, r=rhs, st=(dc == 0), sp=(dc == 7):
                   e.matmul(o[:, :], l, r, start=st, stop=sp), PE if dc == 7 else None)
            pj_wr[w % 4] = PE.v
            W("vector", PE, max(pj_wr[w % 4], kaug_free))
            OP("vector", lambda e, b=bk, w=w: e.tensor_copy(kaugA[0:64, w * 512:(w + 1) * 512], pb[b][0:64, :]))
            ev = OP("vector", lambda e, b=bk, w=w: e.tensor_copy(kaugB[0:64, w * 512:(w + 1) * 512], pb[b][64:128, :]), VE)
            pj_ev[w % 4] = ev
            ev_k[w] = ev
            if pend_rows is not None:
                emit_rows_k(pend_rows)
            pend_rows = w
        emit_rows_k(pend_rows)
        lnk_t = [None, None]
        for hh in range(2):
            W("vector", VE, prowS[0])
            km = OP("vector", lambda e, hh=hh: e.reduce_max(smalls[0:1, 4 + hh:5 + hh], kmaxp[0:1, hh * 8:hh * 8 + 8], axis=AX.X), VE)
            W("scalar", VE, km)
            lnk_t[hh] = OP("scalar", lambda e, hh=hh: e.activation(smalls[0:1, 4 + hh:5 + hh], smalls[0:1, 4 + hh:5 + hh], AF.Ln), SQ)
        W("sync", PE, xq_rd[0])
        xq_t[0] = load_xq(0, 0)
        for dc in range(8):
            if dc < 7:
                W("sync", PE, xq_rd[(dc + 1) % 2])
                xq_t[(dc + 1) % 2] = load_xq((dc + 1) % 2, dc + 1)
            WT("tensor", xq_t[dc % 2])
            if dc == 0:
                W("tensor", SQ, st_rd[0])
                W("tensor", SQ, st_rd[1])
                W("tensor", VE, yev[0])
                W("tensor", VE, yev[1])
                W("tensor", VE, pj_ev[2])
                W("tensor", VE, pj_ev[3])
            for wdw in range(4):
                lhs = wqk[buf][:, dc * 128:(dc + 1) * 128]
                rhs = xqt[dc % 2][:, wdw * 512:(wdw + 1) * 512]
                OP("tensor", lambda e, o=pb[wdw], l=lhs, r=rhs, st=(dc == 0), sp=(dc == 7):
                   e.matmul(o[:, :], l, r, start=st, stop=sp), PE if wdw == 3 else None)
            xq_rd[dc % 2] = PE.v
        qproj_t = PE.v
        wqk_rd[buf] = PE.v
        W("vector", PE, qproj_t)
        ev_w = [None] * 4
        for wdw in range(4):
            OP("vector", lambda e, w=wdw: e.tensor_scalar_mul(qaugA[0:64, w * 512:(w + 1) * 512], pb[w][0:64, :], 0.125))
            ev = OP("vector", lambda e, w=wdw: e.tensor_scalar_mul(qaugB[0:64, w * 512:(w + 1) * 512], pb[w][64:128, :], 0.125), VE)
            ev_w[wdw] = ev
            if wdw == 3:
                kq_ready[0] = kq_ready[1] = ev

        def make_chain(hh, wdw):
            qa_h = qa[hh]

            def emit_chain():
                W("scalar", VE, ev_w[wdw])
                W("scalar", PE, qsq_rd[0])
                sq = OP("scalar", lambda e, q=qa_h, w=wdw:
                        e.activation(qsqw[0:64, :], q[0:64, w * 512:(w + 1) * 512], AF.Square), SQ)
                W("tensor", SQ, sq)
                W("tensor", VE, prowS[0])
                W("tensor", SQ, prow_sq[0])
                rw = OP("tensor", lambda e: e.matmul(prow[0:1, :], onescol[0:64, 0:1], qsqw[0:64, :], start=True, stop=True), PE)
                qsq_rd[0] = rw
                W("scalar", PE, rw)
                W("scalar", VE, qn2_ad[0])
                ql = OP("scalar", lambda e: e.activation(qn2w[0:1, :], prow[0:1, :], AF.Ln), SQ)
                prow_sq[0] = ql
                W("vector", SQ, ql)
                W("vector", SQ, lnk_t[hh])
                ad = OP("vector", lambda e, hh=hh: e.tensor_scalar_add(qn2w[0:1, :], qn2w[0:1, :], smalls[0:1, 4 + hh:5 + hh]), VE)
                qn2_ad[0] = ad
                W("scalar", VE, ad)
                bw = OP("scalar", lambda e, q=qa_h, w=wdw:
                        e.activation(q[64:65, w * 512:(w + 1) * 512], qn2w[0:1, :], AF.Exp, scale=0.5), SQ)
                bound_w[hh][wdw] = bw
                bound_ready[hh] = bw
            return emit_chain

        return [[make_chain(hh, w) for w in range(4)] for hh in range(2)]

    def proj_v(q):
        buf = q % 2
        W("tensor", LXT, LXT.v)
        WT("tensor", wv_t[buf])
        for tk in range(32):
            bk = 4 + (tk % 2)
            W("tensor", VE, pj_ev[tk % 2])
            for dc in range(8):
                lhs = xTt[:, dc * S + tk * 128: dc * S + (tk + 1) * 128]
                rhs = wvt[buf][:, dc * 256:(dc + 1) * 256]
                OP("tensor", lambda e, o=pb[bk], l=lhs, r=rhs, st=(dc == 0), sp=(dc == 7):
                   e.matmul(o[:, 0:256], l, r, start=st, stop=sp), PE if dc == 7 else None)
            pj_wr[tk % 2] = PE.v
            W("vector", PE, pj_wr[tk % 2])
            dst = vqv[:, tk, :, 0:64]
            src = pb[bk][:, 0:256].rearrange("p (h q) -> p h q", h=4)
            ev = OP("vector", lambda e, d=dst, s=src: e.tensor_copy(d, s), VE)
            pj_ev[tk % 2] = ev
        wv_rd[buf] = PE.v
        return VE.v

    def attention(h, chains=None):
        hh = h % 2
        chain_q = []
        if chains is not None:
            order = [(0, 0), (0, 1), (1, 0), (0, 2), (1, 1), (0, 3), (1, 2), (1, 3)]
            chain_q = [(a, b) for (a, b) in order if chains[a][b] is not None]

        def pop_chain():
            if chain_q:
                a, b = chain_q.pop(0)
                chains[a][b]()
                chains[a][b] = None
        if debug:
            W("gpsimd", SQ, bound_ready[hh])
            dma_g(dbg_bnd[h:h + 1, :], (qaugA if hh == 0 else qaugB)[64:65, :])
        ka = kaugA if hh == 0 else kaugB
        qa = qaugA if hh == 0 else qaugB
        vbase = (h % 4) * 65
        steps = []
        for s_ in range(NSLOT):
            for c in range(EXT[s_]):
                steps.append((s_, c, len(steps) % 2))
        exp_done = {}
        msk_done = {}
        pend = None

        def emit_pv2(s_, c, stb, last):
            yb = 2 + (s_ % 2)
            W("tensor", SQ, exp_done[(s_, c)])
            if (s_, c) in msk_done:
                W("tensor", VE, msk_done[(s_, c)])
            if c == 0:
                W("tensor", VE, yev[s_ % 2])
            pv = OP("tensor", lambda e, b=stb, c=c, yb=yb, vb=vbase:
                    e.matmul(pb[yb][0:65, :], vq[:, c * 260 + vb: c * 260 + vb + 65],
                             (pt0 if b == 0 else pt1)[:, :], start=(c == 0), stop=last), PE)
            pt_rd[stb] = pv
            ywr[s_ % 2] = pv

        def emit_epilogue(s_):
            yb = 2 + (s_ % 2)
            W("vector", PE, ywr[s_ % 2])
            W("vector", SQ, lnl_rd[0])
            lv = OP("vector", lambda e, yb=yb: e.tensor_copy(lnl[0:1, :], pb[yb][64:65, :]), VE)
            W("scalar", VE, lv)
            l1 = OP("scalar", lambda e: e.activation(lnl[0:1, :], lnl[0:1, :], AF.Ln), SQ)
            W("scalar", SQ, l1)
            W("scalar", PE, rlr_rd[0])
            l2 = OP("scalar", lambda e: e.activation(rlr[0:1, :], lnl[0:1, :], AF.Exp, scale=-1.0), SQ)
            lnl_rd[0] = l2
            W("tensor", SQ, l2)
            W("tensor", VE, rlS[0])
            bc = OP("tensor", lambda e: e.matmul(pb[6][0:64, :], onesrow[0:1, 0:64], rlr[0:1, :],
                                                 start=True, stop=True), PE)
            rlr_rd[0] = bc
            W("vector", PE, bc)
            cv = OP("vector", lambda e: e.tensor_copy(scrf[0:64, :], pb[6][0:64, :]), VE)
            rlS[0] = cv
            W("vector", VE, cv)
            yev[s_ % 2] = OP("vector", lambda e, yb=yb, h=h, s=s_:
                             e.tensor_mul(y_sb[(h % 2) * 64:(h % 2) * 64 + 64,
                                               (h // 2) * TOK + s * 512:(h // 2) * TOK + (s + 1) * 512],
                                          pb[yb][0:64, :], scrf[0:64, :]), VE)

        for ki, (s_, c, stb) in enumerate(steps):
            while chains is not None and chains[hh][s_] is not None:
                pop_chain()
            ext = EXT[s_]
            W("tensor", SQ, st_rd[stb])
            W("tensor", VE, kq_ready[hh])
            W("tensor", SQ, bound_w[hh][s_] or bound_ready[hh])
            WT("tensor", masks_t)
            W("tensor", VE, consts_t)
            W("tensor", VE, vq_ready)
            mm = OP("tensor", lambda e, b=stb, c=c, s=s_, k=ka, q=qa:
                    e.matmul(pb[b][:, :], k[:, c * 128:(c + 1) * 128], q[:, s * 512:(s + 1) * 512],
                             start=True, stop=True), PE)
            st_wr[stb] = mm
            W("scalar", PE, mm)
            W("scalar", PE, pt_rd[stb])
            ex = OP("scalar", lambda e, b=stb: e.activation((pt0 if b == 0 else pt1)[:, :], pb[b][:, :], AF.Exp), SQ)
            st_rd[stb] = ex
            exp_done[(s_, c)] = ex
            if c >= ext - 8:
                m8 = c - (ext - 8)
                W("vector", SQ, ex)
                mk = OP("vector", lambda e, b=stb, m=m8:
                        e.tensor_mul((pt0 if b == 0 else pt1)[:, :], (pt0 if b == 0 else pt1)[:, :],
                                     masksb[:, m * 512:(m + 1) * 512]), VE)
                msk_done[(s_, c)] = mk
            if pend is not None:
                ps, pc, pstb = pend
                emit_pv2(ps, pc, pstb, last=(pc == EXT[ps] - 1))
                if pc == EXT[ps] - 1:
                    emit_epilogue(ps)
            pend = (s_, c, stb)
            if c in (2, 6):
                pop_chain()
        ps, pc, pstb = pend
        emit_pv2(ps, pc, pstb, last=True)
        emit_epilogue(ps)
        while chain_q:
            pop_chain()
        qaug_rd[hh] = PE.v
        kaug_rd[hh] = PE.v

    vq_ready = proj_v(0)
    chains = proj_pair(0)
    for j in range(8):
        attention(2 * j, chains)
        if j < 7:
            W("gpsimd", PE, wqk_rd[(j + 1) % 2])
            wqk_t[(j + 1) % 2] = load_wqk((j + 1) % 2, j + 1)
            if (j + 1) % 2 == 0:
                W("gpsimd", PE, wv_rd[((j + 1) // 2) % 2])
                wv_t[((j + 1) // 2) % 2] = load_wv(((j + 1) // 2) % 2, (j + 1) // 2)
        attention(2 * j + 1, chains)
        if j < 7:
            if (j + 1) % 2 == 0:
                vq_ready = proj_v((j + 1) // 2)
            chains = proj_pair(j + 1)

    for eng in ENGS:
        if eng in ("gpsimd", "sync"):
            queue_drain(eng)
        OP(eng, (lambda e: e.nop()) if eng in ("sync", "gpsimd", "scalar", "tensor") else (lambda e: e.engine_nop()), PH)
    PH1 = PH.v
    for eng in ENGS:
        W(eng, PH, PH1)
    wgv0 = wguT.rearrange("e (dc p) n -> e p dc n", p=128)
    wg0_t = []
    for hc0 in range(2):
        wg0_t.append([dma_g(wg0[hc0 * 2], wgv0[0, :, :, hc0 * 128:(hc0 + 1) * 128]),
                      dma_g(wg0[hc0 * 2 + 1], wgv0[0, :, :, HID + hc0 * 128:HID + (hc0 + 1) * 128])])


    wot_t = dma_g(wot[:, :], woT.rearrange("(dc p) n -> p dc n", p=128))
    nw_t = dma_s(nwsb[0:1, :], nw[0:1, :])
    al_t = dma_s(alsb[0:1, :], alin[0:1, :])
    io_t = dma_s(r8[:, :], iota[:, :])
    xot_t = [None, None, None]
    for i in range(3):
        xot_t[i] = dma_s(xot[i][:, :], xo[i * 128:(i + 1) * 128, :])

    WT("tensor", nw_t)
    b1 = OP("tensor", lambda e: e.matmul(pb[5][:, :], onesrow[0:1, :], nwsb[0:1, 0:512], start=True, stop=True), PE)
    W("vector", PE, b1)
    OP("vector", lambda e: e.tensor_copy(nwbc[:, 0:512], pb[5][:, :]))
    nb1 = OP("vector", lambda e: e.engine_nop(), VE)
    W("tensor", VE, nb1)
    b2 = OP("tensor", lambda e: e.matmul(pb[5][:, :], onesrow[0:1, :], nwsb[0:1, 512:1024], start=True, stop=True), PE)
    W("vector", PE, b2)
    nwbc_t = OP("vector", lambda e: e.tensor_copy(nwbc[:, 512:1024], pb[5][:, :]), VE)
    WT("vector", al_t)
    a1 = OP("vector", lambda e: e.tensor_scalar_max(alsb[0:1, 0:1], alsb[0:1, 0:1], -0.5), VE)
    W("vector", VE, a1)
    a2 = OP("vector", lambda e: e.tensor_scalar_min(alsb[0:1, 0:1], alsb[0:1, 0:1], 2.0), VE)
    W("vector", VE, a2)
    a2 = OP("vector", lambda e: e.tensor_scalar_mul(alsb[0:1, 0:1], alsb[0:1, 0:1], 0.5), VE)
    W("tensor", VE, a2)
    W("tensor", VE, nwbc_t)
    a3 = OP("tensor", lambda e: e.matmul(pb[5][:, 0:1], onesrow[0:1, :], alsb[0:1, 0:1], start=True, stop=True), PE)
    W("vector", PE, a3)
    albc_t = OP("vector", lambda e: e.tensor_copy(smalls[:, 6:7], pb[5][:, 0:1]), VE)

    wo_wr = [0] * 6
    wo_ev = [0] * 6
    ntok_t = [0, 0]
    ntok_rd = [0, 0]
    tp_t = [None] * NT
    lg_wr = 0
    lg_ev = 0
    xot_rd = [0, 0, 0]

    scr_waw = [0]
    colS0 = cols[:, 0:1]; colS1 = cols[:, 1:2]; colSum = cols[:, 2:3]
    colMean = cols[:, 3:4]; colNV = cols[:, 4:5]; colRstd = cols[:, 5:6]
    colSsq = cols[:, 6:7]; colR2 = cols[:, 7:8]

    for tt in range(NT):
        WT("tensor", wot_t)
        for half in range(2):
            bki = (tt * 2 + half) % 6
            bk = bki
            W("tensor", VE, wo_ev[bki])
            if bki >= 4:
                W("tensor", VE, pj_ev[bki - 4])
                W("tensor", VE, rlS[0])
                if bki == 5:
                    W("tensor", VE, albc_t)
            for ic in range(8):
                lhs = y_sb[:, ic * TOK + tt * 128:ic * TOK + (tt + 1) * 128]
                rhs = wot[:, ic * D + half * 512:ic * D + half * 512 + 512]
                OP("tensor", lambda e, o=pb[bk], l=lhs, r=rhs, st=(ic == 0), sp=(ic == 7):
                   e.matmul(o[:, :], l, r, start=st, stop=sp), PE if ic == 7 else None)
            wo_wr[bki] = PE.v
        WT("vector", xot_t[tt % 3])
        sums = [colS0, colS1]
        for half in range(2):
            bki = (tt * 2 + half) % 6
            bk = bki
            W("vector", PE, wo_wr[bki])
            wo_ev[bki] = OP("vector", lambda e, b=bk, t=tt, h=half, sc=sums[half]:
                            e.scalar_tensor_tensor(acc[:, t * D + h * 512:t * D + h * 512 + 512],
                                                   pb[b][:, :], 1.0, xot[t % 3][:, h * 512:h * 512 + 512],
                                                   ALU.mult, ALU.add, accum_out=sc[:, 0:1]), VE)
        xot_rd[tt % 3] = VE.v
        if tt + 3 < NT:
            W("sync", VE, xot_rd[tt % 3])
            xot_t[(tt + 3) % 3] = dma_s(xot[(tt + 3) % 3][:, :], xo[(tt + 3) * 128:(tt + 4) * 128, :])
        W("vector", VE, wo_ev[(tt * 2) % 6])
        W("vector", VE, wo_ev[(tt * 2 + 1) % 6])
        s1 = OP("vector", lambda e: e.tensor_add(colSum[:, 0:1], colS0[:, 0:1], colS1[:, 0:1]), VE)
        W("vector", VE, s1)
        s2 = OP("vector", lambda e: e.tensor_scalar_mul(colMean[:, 0:1], colSum[:, 0:1], 1.0 / D), VE)
        W("vector", VE, s2)
        W("vector", VE, scr_waw[0])
        OP("vector", lambda e, t=tt: e.scalar_tensor_tensor(scr[:, :], acc[:, t * D:t * D + 512],
                                                            colMean[:, 0:1], acc[:, t * D:t * D + 512],
                                                            ALU.subtract, ALU.mult, accum_out=colNV[:, 0:1]), VE)
        W("vector", VE, VE.v)
        h2 = OP("vector", lambda e, t=tt: e.scalar_tensor_tensor(scr[:, :], acc[:, t * D + 512:(t + 1) * D],
                                                                 colMean[:, 0:1], acc[:, t * D + 512:(t + 1) * D],
                                                                 ALU.subtract, ALU.mult, accum_out=cols[:, 24:25]), VE)
        W("vector", VE, h2)
        s3 = OP("vector", lambda e: e.tensor_add(colNV[:, 0:1], colNV[:, 0:1], cols[:, 24:25]), VE)
        W("vector", VE, s3)
        scr_waw[0] = s3
        W("scalar", VE, s3)
        r1 = OP("scalar", lambda e: e.activation(colRstd[:, 0:1], colNV[:, 0:1], AF.Ln, scale=1.0 / D, bias=eps5), SQ)
        W("scalar", SQ, r1)
        r2 = OP("scalar", lambda e: e.activation(colRstd[:, 0:1], colRstd[:, 0:1], AF.Exp, scale=-0.5), SQ)
        W("vector", SQ, r2)
        n1 = OP("vector", lambda e, t=tt: e.tensor_scalar(acc[:, t * D:(t + 1) * D], acc[:, t * D:(t + 1) * D],
                                                          colMean[:, 0:1], colRstd[:, 0:1], ALU.subtract, ALU.mult), VE)
        W("vector", VE, n1)
        W("vector", VE, scr_waw[0])
        OP("vector", lambda e, t=tt: e.scalar_tensor_tensor(scr[:, :], acc[:, t * D:t * D + 512],
                                                            0.0, acc[:, t * D:t * D + 512],
                                                            ALU.subtract, ALU.mult, accum_out=colSsq[:, 0:1]), VE)
        W("vector", VE, VE.v)
        h2 = OP("vector", lambda e, t=tt: e.scalar_tensor_tensor(scr[:, :], acc[:, t * D + 512:(t + 1) * D],
                                                                 0.0, acc[:, t * D + 512:(t + 1) * D],
                                                                 ALU.subtract, ALU.mult, accum_out=cols[:, 24:25]), VE)
        W("vector", VE, h2)
        s4 = OP("vector", lambda e: e.tensor_add(colSsq[:, 0:1], colSsq[:, 0:1], cols[:, 24:25]), VE)
        W("vector", VE, s4)
        scr_waw[0] = s4
        W("scalar", VE, s4)
        r3 = OP("scalar", lambda e: e.activation(colR2[:, 0:1], colSsq[:, 0:1], AF.Ln, scale=1.0 / D, bias=eps6), SQ)
        W("scalar", SQ, r3)
        r4 = OP("scalar", lambda e: e.activation(colR2[:, 0:1], colR2[:, 0:1], AF.Exp, scale=-0.5), SQ)
        W("vector", SQ, r4)
        OP("vector", lambda e, t=tt: e.tensor_copy(r2s[:, t:t + 1], cols[:, 7:8]))
        W("vector", VE, nwbc_t)
        WT("vector", tp_t[tt - 2] if tt >= 2 else None)
        ntok_t[tt % 2] = OP("vector", lambda e, t=tt: e.scalar_tensor_tensor(ntok[t % 2][:, :], acc[:, t * D:(t + 1) * D],
                                                                             colR2[:, 0:1], nwbc[:, :], ALU.mult, ALU.mult), VE)
        W("sync", VE, ntok_t[tt % 2])
        P.wait("sync", TS[tt % 2], TS[tt % 2].v)
        for dc in range(8):
            tp_t[tt] = dma_fixed("sync", TS[tt % 2], normed[:, dc * TOK + tt * 128:dc * TOK + (tt + 1) * 128],
                                 ntok[tt % 2][:, dc * 128:(dc + 1) * 128], transpose=True)

    y_free = max(wo_wr)
    rt0_tok = dma_s(rtrow[0:1, :], rt_raw[0:1, :])
    for eng in ENGS:
        if eng in ("gpsimd", "sync"):
            queue_drain(eng)
        OP(eng, (lambda e: e.nop()) if eng != "vector" else (lambda e: e.engine_nop()), PH)
    PH2 = PH.v
    for eng in ("sync", "vector", "scalar"):
        W(eng, PH, PH2)
    if debug:
        dy = dma_g(dbg_y[:, :], y_sb[:, :])
        dn = dma_g(dbg_nrm[:, :], normed[:, :])
        da = dma_s(dbg_acc[:, :], acc[:, :])
        dc_ = dma_s(dbg_cmb[:, :], comb[:, :])
        queue_drain("gpsimd")
        queue_drain("sync")
        for eng in ENGS:
            OP(eng, (lambda e: e.nop()) if eng != "vector" else (lambda e: e.engine_nop()), PH)
        PH2b = PH.v
        for eng in ENGS:
            W(eng, PH, PH2b)

    rt_state = {"rt_t": rt0_tok, "rt_rd": 0, "bc_ev": 0, "rtnw_rd": 0, "dots": 0, "comb_done": 0}

    def router_piece(er):
        bws = []
        for half in range(2):
            WT("tensor", rt_state["rt_t"])
            W("tensor", VE, rt_state["bc_ev"])
            W("tensor", VE, wo_ev[5])
            bw = OP("tensor", lambda e, h=half: e.matmul(pb[5][:, :], onesrow[0:1, :],
                    rtrow[0:1, h * 512:(h + 1) * 512], start=True, stop=True), PE)
            bws.append(bw)
            W("vector", PE, bw)
            W("vector", VE, rt_state["rtnw_rd"])
            rt_state["bc_ev"] = OP("vector", lambda e, h=half: e.tensor_mul(rtnw[:, h * 512:(h + 1) * 512],
                                   pb[5][:, :], nwbc[:, h * 512:(h + 1) * 512]), VE)
        rt_state["rt_rd"] = bws[-1]
        if er + 1 < E:
            W("sync", PE, rt_state["rt_rd"])
            rt_state["rt_t"] = dma_s(rtrow[0:1, :], rt_raw[er + 1:er + 2, :])
        for tt in range(NT):
            W("vector", VE, rt_state["bc_ev"])
            W("vector", VE, rt_state["dots"])
            rt_state["dots"] = OP("vector", lambda e, t=tt, e_=er: e.scalar_tensor_tensor(
                dumpf[:, :], acc[:, t * D:(t + 1) * D], 0.0, rtnw[:, :],
                ALU.subtract, ALU.mult, accum_out=lgs[:, t * E + e_:t * E + e_ + 1]), VE)
        rt_state["rtnw_rd"] = rt_state["dots"]

    def router_top2():
        lg = u8[:, 0:E]; uu = u8[:, E:2 * E]; pp = u8[:, 2 * E:3 * E]; eq1 = u8[:, 3 * E:4 * E]
        cand = u8[:, 4 * E:5 * E]; mk1 = u8[:, 5 * E:6 * E]; pp2 = u8[:, 6 * E:7 * E]; mk2 = u8[:, 7 * E:8 * E]
        cM1 = cols[:, 8:9]; cZ = cols[:, 9:10]; cZr = cols[:, 10:11]; cP1 = cols[:, 11:12]
        cFi = cols[:, 12:13]; cM2 = cols[:, 13:14]; cSi = cols[:, 14:15]; cWs = cols[:, 15:16]
        cWr = cols[:, 16:17]; cW1 = cols[:, 17:18]; cW2 = cols[:, 18:19]

        def vop(fn):
            v = OP("vector", fn, VE)
            W("vector", VE, v)
            return v
        for tt in range(NT):
            W("vector", VE, rt_state["dots"])
            WT("vector", io_t)
            W("vector", VE, albc_t)
            vop(lambda e, t=tt: e.tensor_scalar(lg, lgs[:, t * E:(t + 1) * E], r2s[:, t:t + 1], None, ALU.mult))
            vmx = vop(lambda e: e.reduce_max(cM1[:, 0:1], lg, axis=AX.X, negate=True))
            W("scalar", VE, vmx)
            sqe = OP("scalar", lambda e: e.activation(uu, lg, AF.Exp, bias=cM1[:, 0:1]), SQ)
            W("vector", SQ, sqe)
            vop(lambda e: e.reduce_sum(cZ[:, 0:1], uu, axis=AX.X))
            vop(lambda e: e.reciprocal(cZr[:, 0:1], cZ[:, 0:1]))
            vop(lambda e: e.tensor_scalar_mul(pp, uu, cZr[:, 0:1]))
            vop(lambda e: e.reduce_max(cP1[:, 0:1], pp, axis=AX.X))
            vop(lambda e: e.tensor_scalar(eq1, pp, cP1[:, 0:1], None, ALU.is_equal))
            vop(lambda e: e.scalar_tensor_tensor(cand, eq1, -1000.0, r8[:, :], ALU.mult, ALU.add))
            vop(lambda e: e.tensor_reduce(cFi[:, 0:1], cand, axis=AX.X, op=ALU.min))
            vop(lambda e: e.tensor_scalar(mk1, cand, cFi[:, 0:1], None, ALU.is_equal))
            vop(lambda e: e.scalar_tensor_tensor(pp2, mk1, -2.0, pp, ALU.mult, ALU.add))
            vop(lambda e: e.reduce_max(cM2[:, 0:1], pp2, axis=AX.X))
            vop(lambda e: e.tensor_scalar(eq1, pp2, cM2[:, 0:1], None, ALU.is_equal))
            vop(lambda e: e.scalar_tensor_tensor(cand, eq1, -1000.0, r8[:, :], ALU.mult, ALU.add))
            vop(lambda e: e.tensor_reduce(cSi[:, 0:1], cand, axis=AX.X, op=ALU.min))
            vop(lambda e: e.tensor_scalar(mk2, cand, cSi[:, 0:1], None, ALU.is_equal))
            vop(lambda e: e.tensor_add(cWs[:, 0:1], cP1[:, 0:1], cM2[:, 0:1]))
            vop(lambda e: e.reciprocal(cWr[:, 0:1], cWs[:, 0:1]))
            vop(lambda e: e.tensor_mul(cW1[:, 0:1], cP1[:, 0:1], cWr[:, 0:1]))
            vop(lambda e: e.tensor_mul(cW2[:, 0:1], cM2[:, 0:1], cWr[:, 0:1]))
            vop(lambda e: e.tensor_scalar_mul(mk2, mk2, cW2[:, 0:1]))
            vop(lambda e: e.scalar_tensor_tensor(mk1, mk1, cW1[:, 0:1], mk2, ALU.mult, ALU.add))
            rt_state["comb_done"] = vop(lambda e, t=tt: e.tensor_scalar_mul(comb[:, t * E:(t + 1) * E], mk1, smalls[:, 6:7]))

    wgv = wguT.rearrange("e (dc p) n -> e p dc n", p=128)
    wdv = wdT.rearrange("e (hc p) n -> e p hc n", p=128)

    wg_t = {}
    wg_rd = {}
    wd_t = [None] * 8
    wd_rd = 0
    hid_rd = [0, 0]
    hid_wr = [0, 0]
    g_wr = [0, 0]; g_rd = [0, 0]
    u_wr = [0, 0]; u_rd = [0, 0]
    d_wr = [0, 0]; d_ev = [0, 0]
    scr_rd = [0, 0]
    dn_done = [[0] * 2 for _ in range(NT)]
    st_t = [None, None]
    out_rd = [0, 0]

    def load_wg(slot, e_, hc):
        t1 = dma_g(wg[slot * 2][:, :], wgv[e_, :, :, hc * 128:(hc + 1) * 128])
        t2 = dma_g(wg[slot * 2 + 1][:, :], wgv[e_, :, :, HID + hc * 128:HID + (hc + 1) * 128])
        return [t1, t2]

    def load_wd(e_, hc, gi):
        return dma_g(wd[:, (hc - HGRP[gi][0]) * D:(hc - HGRP[gi][0] + 1) * D],
                     wdv[e_, :, hc, :])

    seq = [(e_, gi, hc) for e_ in range(E) for gi in range(3) for hc in range(*HGRP[gi])]
    wg_t[seq[0]] = wg0_t[0]
    wg_t[seq[1]] = wg0_t[1]

    for idx, (e_, gi, hc) in enumerate(seq):
        g0, g1 = HGRP[gi]
        hcl = hc - g0
        hbuf = (e_ * 3 + gi) % 2
        hid = hid0 if hbuf == 0 else hid1
        slot = idx % 2
        W("gpsimd", PE, wd_rd)
        if e_ == 0 and gi == 0 and hcl >= 4:
            W("gpsimd", PE, wg_rd.get(seq[1], 0))
        wd_t[hcl] = load_wd(e_, hc, gi)
        if idx < 2:
            wt_g, wt_u = wg0[hcl * 2], wg0[hcl * 2 + 1]
        else:
            wt_g, wt_u = wg[slot * 2][:, :], wg[slot * 2 + 1][:, :]
        for tq in range(4):
            bg = (hcl * 4 + tq) % 2
            bu = 2 + (hcl * 4 + tq) % 2
            WT("tensor", wg_t[(e_, gi, hc)])
            for k4 in range(4):
                WT("tensor", tp_t[4 * tq + k4])
            W("tensor", SQ, g_rd[bg])
            W("tensor", VE, u_rd[bu - 2])
            W("tensor", VE, wo_ev[bg])
            W("tensor", VE, wo_ev[bu])
            for dc in range(8):
                OP("tensor", lambda e, b=bg, wt=wt_g, dc=dc, tq=tq:
                   e.matmul(pb[b][:, :], wt[:, dc * 128:(dc + 1) * 128],
                            normed[:, dc * TOK + tq * 512:dc * TOK + (tq + 1) * 512],
                            start=(dc == 0), stop=(dc == 7)), PE if dc == 7 else None)
            g_wr[bg] = PE.v
            for dc in range(8):
                OP("tensor", lambda e, b=bu, wt=wt_u, dc=dc, tq=tq:
                   e.matmul(pb[b][:, :], wt[:, dc * 128:(dc + 1) * 128],
                            normed[:, dc * TOK + tq * 512:dc * TOK + (tq + 1) * 512],
                            start=(dc == 0), stop=(dc == 7)), PE if dc == 7 else None)
            u_wr[bu - 2] = PE.v
            sh = (hcl * 4 + tq) % 2
            W("scalar", PE, g_wr[bg])
            W("scalar", VE, scr_rd[sh])
            sl = OP("scalar", lambda e, b=bg, sh=sh: e.activation(sil[sh][:, :], pb[b][:, :], AF.Silu), SQ)
            g_rd[bg] = sl
            W("vector", SQ, sl)
            W("vector", PE, u_wr[bu - 2])
            W("vector", PE, hid_rd[hbuf])
            W("vector", PE, y_free)
            mu = OP("vector", lambda e, b=bu, sh=sh, hid=hid, hcl=hcl, tq=tq:
                    e.tensor_mul(hid[:, hcl * TOK + tq * 512:hcl * TOK + (tq + 1) * 512],
                                 sil[sh][:, :], pb[b][:, :]), VE)
            u_rd[bu - 2] = mu
            scr_rd[sh] = mu
        if e_ == 0 and gi == 0:
            router_piece(hcl)
            if hcl == 7:
                router_top2()
                d_ev[1] = max(d_ev[1], rt_state["bc_ev"])
        if idx == 0:
            d_ev[0] = max(d_ev[0], wo_ev[4])
            d_ev[1] = max(d_ev[1], wo_ev[5])
        hid_wr[hbuf] = VE.v
        if hc == g1 - 1:
            nhc = g1 - g0
            W("tensor", VE, hid_wr[hbuf])
            for hcx in range(nhc):
                WT("tensor", wd_t[hcx])
            for tt in range(NT):
                for half in range(2):
                    bd = 4 + (tt * 2 + half) % 2
                    W("tensor", VE, d_ev[(tt * 2 + half) % 2])
                    for hcx in range(nhc):
                        OP("tensor", lambda e, b=bd, hid=hid, hcx=hcx, tt=tt, hf=half, n_=nhc:
                           e.matmul(pb[b][:, :], hid[:, hcx * TOK + tt * 128:hcx * TOK + (tt + 1) * 128],
                                    wd[:, hcx * D + hf * 512:hcx * D + hf * 512 + 512],
                                    start=(hcx == 0), stop=(hcx == n_ - 1)), PE if hcx == nhc - 1 else None)
                    d_wr[(tt * 2 + half) % 2] = PE.v
                    W("vector", PE, d_wr[(tt * 2 + half) % 2])
                    W("vector", VE, rt_state["comb_done"])
                    d_ev[(tt * 2 + half) % 2] = OP(
                        "vector", lambda e, b=bd, tt=tt, hf=half, e_=e_:
                        e.scalar_tensor_tensor(acc[:, tt * D + hf * 512:tt * D + hf * 512 + 512],
                                               pb[b][:, :], comb[:, tt * E + e_:tt * E + e_ + 1],
                                               acc[:, tt * D + hf * 512:tt * D + hf * 512 + 512],
                                               ALU.mult, ALU.add), VE)
                    dn_done[tt][half] = d_ev[(tt * 2 + half) % 2]
                if e_ == E - 1 and gi == 2:
                    ob = tt % 2
                    W("vector", VE, dn_done[tt][1])
                    WT("vector", st_t[ob])
                    f1 = OP("vector", lambda e, t=tt: e.tensor_reduce(cols[:, 20:21], acc[:, t * D:(t + 1) * D], axis=AX.X, op=ALU.add), VE)
                    W("vector", VE, f1)
                    f2 = OP("vector", lambda e: e.tensor_scalar_mul(cols[:, 21:22], cols[:, 20:21], 1.0 / D), VE)
                    W("vector", VE, f2)
                    W("vector", VE, scr_waw[0])
                    OP("vector", lambda e, t=tt: e.scalar_tensor_tensor(scr[:, :], acc[:, t * D:t * D + 512],
                                                                        cols[:, 21:22], acc[:, t * D:t * D + 512],
                                                                        ALU.subtract, ALU.mult, accum_out=cols[:, 22:23]), VE)
                    W("vector", VE, VE.v)
                    h2 = OP("vector", lambda e, t=tt: e.scalar_tensor_tensor(scr[:, :], acc[:, t * D + 512:(t + 1) * D],
                                                                             cols[:, 21:22], acc[:, t * D + 512:(t + 1) * D],
                                                                             ALU.subtract, ALU.mult, accum_out=cols[:, 24:25]), VE)
                    W("vector", VE, h2)
                    f3 = OP("vector", lambda e: e.tensor_add(cols[:, 22:23], cols[:, 22:23], cols[:, 24:25]), VE)
                    W("vector", VE, f3)
                    scr_waw[0] = f3
                    W("scalar", VE, f3)
                    f4 = OP("scalar", lambda e: e.activation(cols[:, 23:24], cols[:, 22:23], AF.Ln, scale=1.0 / D, bias=smalls[:, 8:9]), SQ)
                    W("scalar", SQ, f4)
                    f5 = OP("scalar", lambda e: e.activation(cols[:, 23:24], cols[:, 23:24], AF.Exp, scale=-0.5), SQ)
                    W("vector", SQ, f5)
                    f6 = OP("vector", lambda e, t=tt, ob=ob: e.tensor_scalar(outsb[ob][:, :], acc[:, t * D:(t + 1) * D],
                                                                             cols[:, 21:22], cols[:, 23:24],
                                                                             ALU.subtract, ALU.mult), VE)
                    W("sync", VE, f6)
                    P.wait("sync", STS[tt % 2], STS[tt % 2].v)
                    st_t[ob] = dma_fixed("sync", STS[tt % 2], out[tt * 128:(tt + 1) * 128, :], outsb[ob][:, :])
            wd_rd = PE.v
            hid_rd[hbuf] = PE.v
        wg_rd[(e_, gi, hc)] = PE.v
        if idx + 2 < len(seq):
            nxt = seq[idx + 2]
            W("gpsimd", PE, wg_rd[(e_, gi, hc)])
            W("gpsimd", VE, albc_t)
            wg_t[nxt] = load_wg(slot, nxt[0], nxt[2])

    queue_drain("sync")

    with nc.Block() as block:
        @block.sync
        def _(e):
            for f in P.ops["sync"]:
                f(e)

        @block.gpsimd
        def _(e):
            for f in P.ops["gpsimd"]:
                f(e)

        @block.vector
        def _(e):
            for f in P.ops["vector"]:
                f(e)

        @block.scalar
        def _(e):
            for f in P.ops["scalar"]:
                f(e)

        @block.tensor
        def _(e):
            for f in P.ops["tensor"]:
                f(e)

    return nc


def kernel(L0_attn_x, L0_attn_W_Q, L0_attn_W_K, L0_attn_W_V, L0_attn_W_O,
           L0_ffn_norm_weight, L0_ffn_router_weight, L0_ffn_gate_up_weight,
           L0_ffn_down_weight, L0_ffn_residual_alpha):
    x = np.ascontiguousarray(np.asarray(L0_attn_x, dtype=np.float32))
    wqT = np.ascontiguousarray(np.asarray(L0_attn_W_Q, np.float32).T)
    wkT = np.ascontiguousarray(np.asarray(L0_attn_W_K, np.float32).T)
    wvT = np.ascontiguousarray(np.asarray(L0_attn_W_V, np.float32).T)
    woT = np.ascontiguousarray(np.asarray(L0_attn_W_O, np.float32).T)
    rt_raw = np.ascontiguousarray(np.asarray(L0_ffn_router_weight, np.float32))
    nw = np.asarray(L0_ffn_norm_weight, np.float32).reshape(1, D)
    al = np.asarray(L0_ffn_residual_alpha, np.float32).reshape(1, 1)
    wguT = np.ascontiguousarray(np.asarray(L0_ffn_gate_up_weight, np.float32).transpose(0, 2, 1))
    wdT = np.ascontiguousarray(np.asarray(L0_ffn_down_weight, np.float32).transpose(0, 2, 1))
    iota = np.broadcast_to(np.arange(E, dtype=np.float32), (128, E)).copy()

    tk = np.arange(128)[:, None]
    tq = np.arange(512)[None, :]
    diag = [(tq >= tk + r * 128).astype(np.float32) for r in range(4)]
    ones = np.ones((128, 512), np.float32)
    zeros = np.zeros((128, 512), np.float32)
    maskA = np.concatenate(diag + [zeros] * 4, axis=0)
    maskB = np.concatenate([ones] * 4 + diag, axis=0)

    xTb = [np.ascontiguousarray(x[b].T) for b in range(B)]
    nc = build_nc()
    in_maps = []
    slots = []
    for c in range(8):
        b, half = c // 2, c % 2
        qbs = [(2 * i + half) * 512 for i in range(4)]
        slots.append((b, qbs))
        xo = np.ascontiguousarray(np.concatenate([x[b, qb:qb + 512] for qb in qbs], axis=0))
        xqv = np.ascontiguousarray(xo.T)
        in_maps.append({
            "xT": xTb[b], "xq": xqv, "xo": xo,
            "wqT": wqT, "wkT": wkT, "wvT": wvT, "woT": woT,
            "rt_raw": rt_raw, "nw": nw, "alin": al,
            "wguT": wguT, "wdT": wdT,
            "masks": maskA if half == 0 else maskB,
            "iota": iota,
        })
    res = run_bass_kernel_spmd(nc, in_maps, core_ids=list(range(8)))
    outp = np.empty((B, S, D), np.float32)
    for c in range(8):
        b, qbs = slots[c]
        oc = res.results[c]["out"]
        for i, qb in enumerate(qbs):
            outp[b, qb:qb + 512] = oc[i * 512:(i + 1) * 512]
    return outp


# version 16 (final)
# speedup vs baseline: 1.0610x; 1.0610x over previous
import numpy as np
import ml_dtypes
import concourse.bass as bass
import concourse.mybir as mybir
from concourse.bass_utils import run_bass_kernel_spmd

F32 = mybir.dt.float32
BF16 = mybir.dt.bfloat16
AF = mybir.ActivationFunctionType
ALU = mybir.AluOpType
AX = mybir.AxisListType

D = 1024
S = 4096
B = 4
H = 16
DH = 64
HID = 2816
E = 8
TOK = 2048
NSLOT = 4
EXT = [8, 16, 24, 32]
HC = HID // 128
HGRP = [(0, 8), (8, 16), (16, HC)]
NT = TOK // 128
DBG_CONST_BOUND = False
ENGS = ("sync", "gpsimd", "vector", "scalar", "tensor")


class Sem:
    def __init__(self, nc, name):
        self.h = nc.alloc_semaphore(name=name)
        self.v = 0
        self.name = name

    def inc(self, n=1):
        self.v += n
        return self.v


class Plan:

    def __init__(self, nc):
        self.nc = nc
        self.ops = {e: [] for e in ENGS}
        self.last_wait = {}

    def emit(self, eng, fn, sem=None, n=1):
        if sem is None:
            self.ops[eng].append(fn)
            return None
        v = sem.inc(n)
        self.ops[eng].append(lambda e, fn=fn, s=sem.h, n=n: fn(e).then_inc(s, n))
        return v

    def wait(self, eng, sem, val):
        if val is None or val <= 0:
            return
        assert val <= sem.v, f"wait on future: {eng} waits {sem.name}>={val} but planned only {sem.v}"
        key = (eng, sem.name)
        if self.last_wait.get(key, -1) >= val:
            return
        self.last_wait[key] = val
        self.ops[eng].append(lambda e, s=sem.h, v=val: e.wait_ge(s, v))


def build_nc(debug=False):
    nc = bass.Bass("TRN2", target_bir_lowering=False)

    xT = nc.dram_tensor("xT", [D, S], F32, kind="ExternalInput").ap()
    xq = nc.dram_tensor("xq", [D, TOK], F32, kind="ExternalInput").ap()
    xo = nc.dram_tensor("xo", [TOK, D], F32, kind="ExternalInput").ap()
    wqT = nc.dram_tensor("wqT", [D, D], F32, kind="ExternalInput").ap()
    wkT = nc.dram_tensor("wkT", [D, D], F32, kind="ExternalInput").ap()
    wvT = nc.dram_tensor("wvT", [D, D], F32, kind="ExternalInput").ap()
    woT = nc.dram_tensor("woT", [D, D], F32, kind="ExternalInput").ap()
    rt_raw = nc.dram_tensor("rt_raw", [E, D], F32, kind="ExternalInput").ap()
    nw = nc.dram_tensor("nw", [1, D], F32, kind="ExternalInput").ap()
    alin = nc.dram_tensor("alin", [1, 1], F32, kind="ExternalInput").ap()
    wguT = nc.dram_tensor("wguT", [E, D, 2 * HID], F32, kind="ExternalInput").ap()
    wdT = nc.dram_tensor("wdT", [E, HID, D], F32, kind="ExternalInput").ap()
    masks = nc.dram_tensor("masks", [8 * 128, 512], F32, kind="ExternalInput").ap()
    iota = nc.dram_tensor("iota", [128, E], F32, kind="ExternalInput").ap()
    xqst = nc.dram_tensor("xqst", [D, TOK], BF16, kind="Internal").ap()
    out = nc.dram_tensor("out", [TOK, D], F32, kind="ExternalOutput").ap()
    if debug:
        dbg_y = nc.dram_tensor("dbg_y", [128, 8 * TOK], F32, kind="ExternalOutput").ap()
        dbg_acc = nc.dram_tensor("dbg_acc", [128, NT * D], F32, kind="ExternalOutput").ap()
        dbg_nrm = nc.dram_tensor("dbg_nrm", [128, 8 * TOK], F32, kind="ExternalOutput").ap()
        dbg_cmb = nc.dram_tensor("dbg_cmb", [128, NT * E], F32, kind="ExternalOutput").ap()
        dbg_bnd = nc.dram_tensor("dbg_bnd", [16, TOK], F32, kind="ExternalOutput").ap()
        dbg_l = nc.dram_tensor("dbg_l", [16, TOK], F32, kind="ExternalOutput").ap()

    A0 = (nc.sbuf_base + 63) // 64 * 64
    TOPB = nc.sbuf_top
    cursor = [A0]
    used = []

    def place(name, shape, dt, region):
        nbytes = int(np.prod(shape[1:])) * (2 if dt == BF16 else 4)
        off = (cursor[0] + 63) // 64 * 64
        cursor[0] = off + nbytes
        assert cursor[0] <= TOPB, f"SBUF overflow at {name}: {cursor[0]} > {TOPB}"
        used.append((name, off, nbytes, region))
        return nc.alloc_sbuf_tensor_at(name, shape, dt, offset=off).ap()

    def place_at(name, shape, dt, off):
        nbytes = int(np.prod(shape[1:])) * (2 if dt == BF16 else 4)
        assert off + nbytes <= TOPB, f"SBUF overflow at {name}"
        return nc.alloc_sbuf_tensor_at(name, shape, dt, offset=off).ap()

    smalls = place("smalls", [128, 32], F32, "G")
    cols = place("cols", [128, 32], F32, "G")
    onesrow = place("onesrow", [1, 128], F32, "G")
    onescol = place("onescol", [128, 1], BF16, "G")
    scr = place("scr", [128, 512], BF16, "G")
    comb = place("comb", [128, NT * E], F32, "G")
    r8 = place("r8", [128, E], F32, "G")
    u8 = place("u8", [128, 8 * E], F32, "G")
    lgs = place("lgs", [128, NT * E], F32, "G")
    r2s = place("r2s", [128, NT], F32, "G")
    rtrow = place("rtrow", [1, D], F32, "G")
    nwbc = place("nwbc", [128, D], F32, "G")

    oY = cursor[0]
    y_sb = place("y_sb", [128, 8 * TOK], BF16, "Y")
    hid0 = place_at("hid0", [128, 8 * TOK], BF16, oY)

    oR2 = cursor[0]
    xTt = place("xTt", [128, 8 * S], BF16, "R2")
    acc = place_at("acc", [128, NT * D], F32, oR2)

    oR3 = cursor[0]
    vq = place("vq", [128, 32 * 4 * 65], BF16, "R3")
    kaugA = place("kaugA", [65, S], BF16, "R3")
    kaugB = place("kaugB", [65, S], BF16, "R3")
    normed = place_at("normed", [128, 8 * TOK], BF16, oR3)
    cursor[0] = max(cursor[0], oR3 + 8 * TOK * 2)

    oR4 = cursor[0]
    qaugA = place("qaugA", [65, TOK], BF16, "R4")
    qaugB = place("qaugB", [65, TOK], BF16, "R4")
    pt0 = place("pt0", [128, 512], BF16, "R4")
    pt1 = place("pt1", [128, 512], BF16, "R4")
    qsqw = place("qsqw", [128, 512], BF16, "R4")
    ksqw = place("ksqw", [128, 512], BF16, "R4")
    scrf = place("scrf", [128, 512], F32, "R4")
    wqk = [place(f"wqk{i}", [128, 2 * 8 * 128], BF16, "R4") for i in range(2)]
    wvt = [place(f"wvt{i}", [128, 8 * 256], BF16, "R4") for i in range(2)]
    masksb = place("masksb", [128, 8 * 512], BF16, "R4")
    xqt = [place(f"xqt{i}", [128, TOK], BF16, "R4") for i in range(2)]
    qn2w = place("qn2w", [1, 512], F32, "R4")
    kmaxp = place("kmaxp", [1, 16], F32, "R4")
    lnl = place("lnl", [1, 512], F32, "R4")
    rlr = place("rlr", [1, 512], F32, "R4")
    endA = cursor[0]
    cursor[0] = oR4
    wot = place("wotB", [128, 8 * D], BF16, "R4B")
    xot = [place(f"xot{i}", [128, D], F32, "R4B") for i in range(3)]
    ntok = [place(f"ntok{i}", [128, D], BF16, "R4B") for i in range(2)]

    nwsb = place("nwsb", [1, D], F32, "R4B")
    alsb = place("alsb", [1, 1], F32, "R4B")
    endB = cursor[0]
    cursor[0] = oR4
    hid1 = place("hid1", [128, 8 * TOK], BF16, "R4D")
    wg = [place(f"wg{i}", [128, 8 * 128], BF16, "R4D") for i in range(4)]
    outsb = [place(f"outsb{i}", [128, D], F32, "R4D") for i in range(2)]
    sil = [place(f"sil{i}", [128, 512], F32, "R4D") for i in range(2)]
    o_out0 = [u for u in used if u[0] == "outsb0"][0][1]
    o_out1 = [u for u in used if u[0] == "outsb1"][0][1]
    rtnw = place_at("rtnw", [128, D], F32, o_out0)
    dumpf = place_at("dumpf", [128, D], F32, o_out1)
    endD = cursor[0]
    cursor[0] = max(endA, endB, endD)

    wd = place("wd", [128, 8 * D], BF16, "R5")
    wg0 = [wd[:, (4 + i) * D:(5 + i) * D] for i in range(4)]

    assert cursor[0] <= TOPB, f"total SBUF {cursor[0] - A0} over budget"

    pb = [nc.alloc_psum_tensor(f"pb{i}", [128, 512], F32).ap() for i in range(7)]
    prow = nc.alloc_psum_tensor("prow", [1, 512], F32).ap()

    GP = [Sem(nc, f"GP{i}") for i in range(20)]
    SY = [Sem(nc, f"SY{i}") for i in range(8)]
    LXT = Sem(nc, "LXT")
    LMK = Sem(nc, "LMK")
    TS = [Sem(nc, f"TS{i}") for i in range(2)]
    STS = [Sem(nc, f"STS{i}") for i in range(2)]
    gp_i = [0]
    sy_i = [0]
    PE = Sem(nc, "PE")
    SQ = Sem(nc, "SQ")
    VE = Sem(nc, "VE")
    PH = Sem(nc, "PH")

    P = Plan(nc)
    W = P.wait
    OP = P.emit

    def WT(eng, tok):
        if tok is None:
            return
        if isinstance(tok, list):
            for t in tok:
                WT(eng, t)
            return
        P.wait(eng, tok[0], tok[1])

    def dma_pool(eng, pool, idx, o, i, **kw):
        s = pool[idx[0] % len(pool)]
        idx[0] += 1
        P.wait(eng, s, s.v)
        v = s.inc(16)
        P.ops[eng].append(lambda e, o=o, i=i, s=s.h, kw=kw: e.dma_start(out=o, in_=i, **kw).then_inc(s, 16))
        return (s, v)

    def dma_g(o, i, **kw):
        return dma_pool("gpsimd", GP, gp_i, o, i, **kw)

    def dma_s(o, i, **kw):
        return dma_pool("sync", SY, sy_i, o, i, **kw)

    def dma_fixed(eng, s, o, i, **kw):
        v = s.inc(16)
        P.ops[eng].append(lambda e, o=o, i=i, s=s.h, kw=kw: e.dma_start(out=o, in_=i, **kw).then_inc(s, 16))
        return (s, v)

    def queue_drain(eng):
        pools = {"gpsimd": GP + [LXT, LMK], "sync": SY + TS + STS}[eng]
        for s in pools:
            if s.v:
                P.wait(eng, s, s.v)

    eps5 = smalls[:, 0:1]
    eps6 = smalls[:, 1:2]

    for dc in range(8):
        xTt_tok = dma_fixed("gpsimd", LXT, xTt[:, dc * S:(dc + 1) * S], xT[dc * 128:(dc + 1) * 128, :])
    def load_wqk(buf, j):
        t1 = dma_g(wqk[buf][:, 0:1024],
                   wqT.rearrange("(dc p) n -> p dc n", p=128)[:, :, j * 128:(j + 1) * 128])
        t2 = dma_g(wqk[buf][:, 1024:2048],
                   wkT.rearrange("(dc p) n -> p dc n", p=128)[:, :, j * 128:(j + 1) * 128])
        return [t1, t2]

    def load_wv(buf, q):
        return dma_g(wvt[buf][:, :],
                     wvT.rearrange("(dc p) n -> p dc n", p=128)[:, :, q * 256:(q + 1) * 256])

    xq_stage_t = []

    def stage_xq():
        for dc in range(8):
            xq_stage_t.append(dma_g(xqst[dc * 128:(dc + 1) * 128, :], xq[dc * 128:(dc + 1) * 128, :]))

    def load_xq(buf, dc):
        WT("sync", xq_stage_t[dc])
        return dma_s(xqt[buf][:, :], xqst[dc * 128:(dc + 1) * 128, :])

    stage_xq()
    wqk_t = [load_wqk(0, 0), None]
    wv_t = [load_wv(0, 0), None]
    masks_t = dma_fixed("gpsimd", LMK, masksb[:, :], masks.rearrange("(c p) n -> p c n", p=128))

    OP("vector", lambda e: e.memset(onesrow[:, :], 1.0))
    OP("vector", lambda e: e.memset(onescol[:, :], 1.0))
    OP("vector", lambda e: e.memset(smalls[:, 0:1], 1e-5))
    OP("vector", lambda e: e.memset(smalls[:, 1:2], 1e-6))
    OP("vector", lambda e: e.memset(smalls[:, 7:8], 2.9957323))
    OP("vector", lambda e: e.memset(smalls[:, 8:9], 2.5e-6))
    OP("vector", lambda e: e.memset(kaugA[64:65, :], -1.0))
    OP("vector", lambda e: e.memset(kaugB[64:65, :], -1.0))
    vqv = vq.rearrange("p (c h q) -> p c h q", c=32, h=4)
    OP("vector", lambda e: e.memset(vqv[:, :, :, 64:65], 1.0))
    consts_t = OP("vector", lambda e: e.engine_nop(), VE)

    pt_t = [None, None]
    pt_rd = [0, 0]
    st_rd = [0, 0]
    st_wr = [0, 0]
    yev = [0, 0]
    ywr = [0, 0]
    pj_ev = [0, 0, 0, 0]
    pj_wr = [0, 0, 0, 0]
    PJB = [4, 5, 0, 1]
    prowS = [0]
    prow_sq = [0]
    ksq_rd = [0]
    qsq_rd = [0]
    lnl_rd = [0]
    rlr_rd = [0]
    prow2S = [0]
    prow2_sq = [0]
    qn2_ad = [0]
    bound_w = [[0] * 4, [0] * 4]
    rlS = [0]
    kq_ready = [0, 0]
    bound_ready = [0, 0]
    vq_ready = 0
    vq_rd = 0
    xq_t = [None, None]
    xq_rd = [0, 0]
    wqk_rd = [0, 0]
    wv_rd = [0, 0]
    qaug_rd = [0, 0]
    kaug_rd = [0, 0]

    def proj_pair(j):
        buf = j % 2
        hA, hB = 2 * j, 2 * j + 1
        qa = [qaugA, qaugB]
        ka = [kaugA, kaugB]
        WT("tensor", wqk_t[buf])
        W("tensor", LXT, LXT.v)
        kaug_free = max(kaug_rd[0], kaug_rd[1], qaug_rd[0], qaug_rd[1])
        ev_k = [None] * 8
        pend_rows = None

        def emit_rows_k(w):
            for hh in range(2):
                W("scalar", VE, ev_k[w])
                W("scalar", PE, ksq_rd[0])
                sq = OP("scalar", lambda e, w=w, hh=hh:
                        e.activation(ksqw[0:64, :], (kaugA if hh == 0 else kaugB)[0:64, w * 512:(w + 1) * 512], AF.Square), SQ)
                W("tensor", SQ, sq)
                W("tensor", VE, prowS[0])
                W("tensor", SQ, prow_sq[0])
                rw = OP("tensor", lambda e: e.matmul(prow[0:1, :], onescol[0:64, 0:1], ksqw[0:64, :], start=True, stop=True), PE)
                ksq_rd[0] = rw
                prow_sq[0] = prow_sq[0]
                W("vector", PE, rw)
                prowS[0] = OP("vector", lambda e, w=w, hh=hh:
                              e.reduce_max(kmaxp[0:1, hh * 8 + w:hh * 8 + w + 1], prow[0:1, :], axis=AX.X), VE)

        for w in range(8):
            bk = PJB[w % 4]
            W("tensor", VE, pj_ev[w % 4])
            if w % 4 >= 2:
                W("tensor", SQ, st_rd[w % 4 - 2])
            for dc in range(8):
                lhs = wqk[buf][:, 1024 + dc * 128:1024 + (dc + 1) * 128]
                rhs = xTt[:, dc * S + w * 512: dc * S + (w + 1) * 512]
                OP("tensor", lambda e, o=pb[bk], l=lhs, r=rhs, st=(dc == 0), sp=(dc == 7):
                   e.matmul(o[:, :], l, r, start=st, stop=sp), PE if dc == 7 else None)
            pj_wr[w % 4] = PE.v
            W("vector", PE, max(pj_wr[w % 4], kaug_free))
            OP("vector", lambda e, b=bk, w=w: e.tensor_copy(kaugA[0:64, w * 512:(w + 1) * 512], pb[b][0:64, :]))
            ev = OP("vector", lambda e, b=bk, w=w: e.tensor_copy(kaugB[0:64, w * 512:(w + 1) * 512], pb[b][64:128, :]), VE)
            pj_ev[w % 4] = ev
            ev_k[w] = ev
            if pend_rows is not None:
                emit_rows_k(pend_rows)
            pend_rows = w
        emit_rows_k(pend_rows)
        lnk_t = [None, None]
        for hh in range(2):
            W("vector", VE, prowS[0])
            km = OP("vector", lambda e, hh=hh: e.reduce_max(smalls[0:1, 4 + hh:5 + hh], kmaxp[0:1, hh * 8:hh * 8 + 8], axis=AX.X), VE)
            W("scalar", VE, km)
            lnk_t[hh] = OP("scalar", lambda e, hh=hh: e.activation(smalls[0:1, 4 + hh:5 + hh], smalls[0:1, 4 + hh:5 + hh], AF.Ln), SQ)
        W("sync", PE, xq_rd[0])
        xq_t[0] = load_xq(0, 0)
        for dc in range(8):
            if dc < 7:
                W("sync", PE, xq_rd[(dc + 1) % 2])
                xq_t[(dc + 1) % 2] = load_xq((dc + 1) % 2, dc + 1)
            WT("tensor", xq_t[dc % 2])
            if dc == 0:
                W("tensor", SQ, st_rd[0])
                W("tensor", SQ, st_rd[1])
                W("tensor", VE, yev[0])
                W("tensor", VE, yev[1])
                W("tensor", VE, pj_ev[2])
                W("tensor", VE, pj_ev[3])
            for wdw in range(4):
                lhs = wqk[buf][:, dc * 128:(dc + 1) * 128]
                rhs = xqt[dc % 2][:, wdw * 512:(wdw + 1) * 512]
                OP("tensor", lambda e, o=pb[wdw], l=lhs, r=rhs, st=(dc == 0), sp=(dc == 7):
                   e.matmul(o[:, :], l, r, start=st, stop=sp), PE if wdw == 3 else None)
            xq_rd[dc % 2] = PE.v
        qproj_t = PE.v
        wqk_rd[buf] = PE.v
        W("vector", PE, qproj_t)
        ev_w = [None] * 4
        for wdw in range(4):
            OP("vector", lambda e, w=wdw: e.tensor_scalar_mul(qaugA[0:64, w * 512:(w + 1) * 512], pb[w][0:64, :], 0.125))
            ev = OP("vector", lambda e, w=wdw: e.tensor_scalar_mul(qaugB[0:64, w * 512:(w + 1) * 512], pb[w][64:128, :], 0.125), VE)
            ev_w[wdw] = ev
            if wdw == 3:
                kq_ready[0] = kq_ready[1] = ev

        def make_chain(hh, wdw):
            qa_h = qa[hh]

            def emit_chain():
                W("scalar", VE, ev_w[wdw])
                W("scalar", PE, qsq_rd[0])
                sq = OP("scalar", lambda e, q=qa_h, w=wdw:
                        e.activation(qsqw[0:64, :], q[0:64, w * 512:(w + 1) * 512], AF.Square), SQ)
                W("tensor", SQ, sq)
                W("tensor", VE, prowS[0])
                W("tensor", SQ, prow_sq[0])
                rw = OP("tensor", lambda e: e.matmul(prow[0:1, :], onescol[0:64, 0:1], qsqw[0:64, :], start=True, stop=True), PE)
                qsq_rd[0] = rw
                W("scalar", PE, rw)
                W("scalar", VE, qn2_ad[0])
                ql = OP("scalar", lambda e: e.activation(qn2w[0:1, :], prow[0:1, :], AF.Ln), SQ)
                prow_sq[0] = ql
                W("vector", SQ, ql)
                W("vector", SQ, lnk_t[hh])
                ad = OP("vector", lambda e, hh=hh: e.tensor_scalar_add(qn2w[0:1, :], qn2w[0:1, :], smalls[0:1, 4 + hh:5 + hh]), VE)
                qn2_ad[0] = ad
                W("scalar", VE, ad)
                bw = OP("scalar", lambda e, q=qa_h, w=wdw:
                        e.activation(q[64:65, w * 512:(w + 1) * 512], qn2w[0:1, :], AF.Exp, scale=0.5), SQ)
                bound_w[hh][wdw] = bw
                bound_ready[hh] = bw
            return emit_chain

        return [[make_chain(hh, w) for w in range(4)] for hh in range(2)]

    def proj_v(q):
        buf = q % 2
        W("tensor", LXT, LXT.v)
        WT("tensor", wv_t[buf])
        for tk in range(32):
            bk = 4 + (tk % 2)
            W("tensor", VE, pj_ev[tk % 2])
            for dc in range(8):
                lhs = xTt[:, dc * S + tk * 128: dc * S + (tk + 1) * 128]
                rhs = wvt[buf][:, dc * 256:(dc + 1) * 256]
                OP("tensor", lambda e, o=pb[bk], l=lhs, r=rhs, st=(dc == 0), sp=(dc == 7):
                   e.matmul(o[:, 0:256], l, r, start=st, stop=sp), PE if dc == 7 else None)
            pj_wr[tk % 2] = PE.v
            W("vector", PE, pj_wr[tk % 2])
            dst = vqv[:, tk, :, 0:64]
            src = pb[bk][:, 0:256].rearrange("p (h q) -> p h q", h=4)
            ev = OP("vector", lambda e, d=dst, s=src: e.tensor_copy(d, s), VE)
            pj_ev[tk % 2] = ev
        wv_rd[buf] = PE.v
        return VE.v

    def attention(h, chains=None):
        hh = h % 2
        chain_q = []
        if chains is not None:
            order = [(0, 0), (0, 1), (1, 0), (0, 2), (1, 1), (0, 3), (1, 2), (1, 3)]
            chain_q = [(a, b) for (a, b) in order if chains[a][b] is not None]

        def pop_chain():
            if chain_q:
                a, b = chain_q.pop(0)
                chains[a][b]()
                chains[a][b] = None
        if debug:
            W("gpsimd", SQ, bound_ready[hh])
            dma_g(dbg_bnd[h:h + 1, :], (qaugA if hh == 0 else qaugB)[64:65, :])
        ka = kaugA if hh == 0 else kaugB
        qa = qaugA if hh == 0 else qaugB
        vbase = (h % 4) * 65
        steps = []
        for s_ in range(NSLOT):
            for c in range(EXT[s_]):
                steps.append((s_, c, len(steps) % 2))
        exp_done = {}
        msk_done = {}
        pend = None

        def emit_pv2(s_, c, stb, last):
            yb = 2 + (s_ % 2)
            W("tensor", SQ, exp_done[(s_, c)])
            if (s_, c) in msk_done:
                W("tensor", VE, msk_done[(s_, c)])
            if c == 0:
                W("tensor", VE, yev[s_ % 2])
            pv = OP("tensor", lambda e, b=stb, c=c, yb=yb, vb=vbase:
                    e.matmul(pb[yb][0:65, :], vq[:, c * 260 + vb: c * 260 + vb + 65],
                             (pt0 if b == 0 else pt1)[:, :], start=(c == 0), stop=last), PE)
            pt_rd[stb] = pv
            ywr[s_ % 2] = pv

        def emit_epilogue(s_):
            yb = 2 + (s_ % 2)
            W("vector", PE, ywr[s_ % 2])
            W("vector", SQ, lnl_rd[0])
            lv = OP("vector", lambda e, yb=yb: e.tensor_copy(lnl[0:1, :], pb[yb][64:65, :]), VE)
            W("scalar", VE, lv)
            l1 = OP("scalar", lambda e: e.activation(lnl[0:1, :], lnl[0:1, :], AF.Ln), SQ)
            W("scalar", SQ, l1)
            W("scalar", PE, rlr_rd[0])
            l2 = OP("scalar", lambda e: e.activation(rlr[0:1, :], lnl[0:1, :], AF.Exp, scale=-1.0), SQ)
            lnl_rd[0] = l2
            W("tensor", SQ, l2)
            W("tensor", VE, rlS[0])
            bc = OP("tensor", lambda e: e.matmul(pb[6][0:64, :], onesrow[0:1, 0:64], rlr[0:1, :],
                                                 start=True, stop=True), PE)
            rlr_rd[0] = bc
            W("vector", PE, bc)
            cv = OP("vector", lambda e: e.tensor_copy(scrf[0:64, :], pb[6][0:64, :]), VE)
            rlS[0] = cv
            W("vector", VE, cv)
            yev[s_ % 2] = OP("vector", lambda e, yb=yb, h=h, s=s_:
                             e.tensor_mul(y_sb[(h % 2) * 64:(h % 2) * 64 + 64,
                                               (h // 2) * TOK + s * 512:(h // 2) * TOK + (s + 1) * 512],
                                          pb[yb][0:64, :], scrf[0:64, :]), VE)

        for ki, (s_, c, stb) in enumerate(steps):
            while chains is not None and chains[hh][s_] is not None:
                pop_chain()
            ext = EXT[s_]
            W("tensor", SQ, st_rd[stb])
            W("tensor", VE, kq_ready[hh])
            W("tensor", SQ, bound_w[hh][s_] or bound_ready[hh])
            WT("tensor", masks_t)
            W("tensor", VE, consts_t)
            W("tensor", VE, vq_ready)
            mm = OP("tensor", lambda e, b=stb, c=c, s=s_, k=ka, q=qa:
                    e.matmul(pb[b][:, :], k[:, c * 128:(c + 1) * 128], q[:, s * 512:(s + 1) * 512],
                             start=True, stop=True), PE)
            st_wr[stb] = mm
            W("scalar", PE, mm)
            W("scalar", PE, pt_rd[stb])
            ex = OP("scalar", lambda e, b=stb: e.activation((pt0 if b == 0 else pt1)[:, :], pb[b][:, :], AF.Exp), SQ)
            st_rd[stb] = ex
            exp_done[(s_, c)] = ex
            if c >= ext - 8:
                m8 = c - (ext - 8)
                W("vector", SQ, ex)
                mk = OP("vector", lambda e, b=stb, m=m8:
                        e.tensor_mul((pt0 if b == 0 else pt1)[:, :], (pt0 if b == 0 else pt1)[:, :],
                                     masksb[:, m * 512:(m + 1) * 512]), VE)
                msk_done[(s_, c)] = mk
            if pend is not None:
                ps, pc, pstb = pend
                emit_pv2(ps, pc, pstb, last=(pc == EXT[ps] - 1))
                if pc == EXT[ps] - 1:
                    emit_epilogue(ps)
            pend = (s_, c, stb)
            if c in (2, 6):
                pop_chain()
        ps, pc, pstb = pend
        emit_pv2(ps, pc, pstb, last=True)
        emit_epilogue(ps)
        while chain_q:
            pop_chain()
        qaug_rd[hh] = PE.v
        kaug_rd[hh] = PE.v

    vq_ready = proj_v(0)
    chains = proj_pair(0)
    for j in range(8):
        attention(2 * j, chains)
        if j < 7:
            W("gpsimd", PE, wqk_rd[(j + 1) % 2])
            wqk_t[(j + 1) % 2] = load_wqk((j + 1) % 2, j + 1)
            if (j + 1) % 2 == 0:
                W("gpsimd", PE, wv_rd[((j + 1) // 2) % 2])
                wv_t[((j + 1) // 2) % 2] = load_wv(((j + 1) // 2) % 2, (j + 1) // 2)
        attention(2 * j + 1, chains)
        if j < 7:
            if (j + 1) % 2 == 0:
                vq_ready = proj_v((j + 1) // 2)
            chains = proj_pair(j + 1)

    for eng in ENGS:
        if eng in ("gpsimd", "sync"):
            queue_drain(eng)
        OP(eng, (lambda e: e.nop()) if eng in ("sync", "gpsimd", "scalar", "tensor") else (lambda e: e.engine_nop()), PH)
    PH1 = PH.v
    for eng in ENGS:
        W(eng, PH, PH1)
    wgv0 = wguT.rearrange("e (dc p) n -> e p dc n", p=128)
    wg0_t = []
    for hc0 in range(2):
        wg0_t.append([dma_g(wg0[hc0 * 2], wgv0[0, :, :, hc0 * 128:(hc0 + 1) * 128]),
                      dma_g(wg0[hc0 * 2 + 1], wgv0[0, :, :, HID + hc0 * 128:HID + (hc0 + 1) * 128])])


    wot_t = dma_g(wot[:, :], woT.rearrange("(dc p) n -> p dc n", p=128))
    nw_t = dma_s(nwsb[0:1, :], nw[0:1, :])
    al_t = dma_s(alsb[0:1, :], alin[0:1, :])
    io_t = dma_s(r8[:, :], iota[:, :])
    xot_t = [None, None, None]
    for i in range(3):
        xot_t[i] = dma_s(xot[i][:, :], xo[i * 128:(i + 1) * 128, :])

    WT("tensor", nw_t)
    b1 = OP("tensor", lambda e: e.matmul(pb[5][:, :], onesrow[0:1, :], nwsb[0:1, 0:512], start=True, stop=True), PE)
    W("vector", PE, b1)
    OP("vector", lambda e: e.tensor_copy(nwbc[:, 0:512], pb[5][:, :]))
    nb1 = OP("vector", lambda e: e.engine_nop(), VE)
    W("tensor", VE, nb1)
    b2 = OP("tensor", lambda e: e.matmul(pb[5][:, :], onesrow[0:1, :], nwsb[0:1, 512:1024], start=True, stop=True), PE)
    W("vector", PE, b2)
    nwbc_t = OP("vector", lambda e: e.tensor_copy(nwbc[:, 512:1024], pb[5][:, :]), VE)
    WT("vector", al_t)
    a1 = OP("vector", lambda e: e.tensor_scalar_max(alsb[0:1, 0:1], alsb[0:1, 0:1], -0.5), VE)
    W("vector", VE, a1)
    a2 = OP("vector", lambda e: e.tensor_scalar_min(alsb[0:1, 0:1], alsb[0:1, 0:1], 2.0), VE)
    W("vector", VE, a2)
    a2 = OP("vector", lambda e: e.tensor_scalar_mul(alsb[0:1, 0:1], alsb[0:1, 0:1], 0.5), VE)
    W("tensor", VE, a2)
    W("tensor", VE, nwbc_t)
    a3 = OP("tensor", lambda e: e.matmul(pb[5][:, 0:1], onesrow[0:1, :], alsb[0:1, 0:1], start=True, stop=True), PE)
    W("vector", PE, a3)
    albc_t = OP("vector", lambda e: e.tensor_copy(smalls[:, 6:7], pb[5][:, 0:1]), VE)

    wo_wr = [0] * 6
    wo_ev = [0] * 6
    ntok_t = [0, 0]
    ntok_rd = [0, 0]
    tp_t = [None] * NT
    lg_wr = 0
    lg_ev = 0
    xot_rd = [0, 0, 0]

    scr_waw = [0]

    for tt in range(NT):
        WT("tensor", wot_t)
        for half in range(2):
            bki = (tt * 2 + half) % 6
            bk = bki
            W("tensor", VE, wo_ev[bki])
            if bki >= 4:
                W("tensor", VE, pj_ev[bki - 4])
                W("tensor", VE, rlS[0])
                if bki == 5:
                    W("tensor", VE, albc_t)
            for ic in range(8):
                lhs = y_sb[:, ic * TOK + tt * 128:ic * TOK + (tt + 1) * 128]
                rhs = wot[:, ic * D + half * 512:ic * D + half * 512 + 512]
                OP("tensor", lambda e, o=pb[bk], l=lhs, r=rhs, st=(ic == 0), sp=(ic == 7):
                   e.matmul(o[:, :], l, r, start=st, stop=sp), PE if ic == 7 else None)
            wo_wr[bki] = PE.v
        WT("vector", xot_t[tt % 3])
        P0 = (tt % 2) * 8
        cS0 = cols[:, P0 + 0:P0 + 1]; cS1 = cols[:, P0 + 1:P0 + 2]; cSum = cols[:, P0 + 2:P0 + 3]
        cMean = cols[:, P0 + 3:P0 + 4]; cNV = cols[:, P0 + 4:P0 + 5]; cRstd = cols[:, P0 + 5:P0 + 6]
        cSsq = cols[:, P0 + 6:P0 + 7]; cR2 = cols[:, P0 + 7:P0 + 8]; cH2 = cols[:, 16 + tt % 2:17 + tt % 2]
        sums = [cS0, cS1]
        for half in range(2):
            bki = (tt * 2 + half) % 6
            bk = bki
            W("vector", PE, wo_wr[bki])
            wo_ev[bki] = OP("vector", lambda e, b=bk, t=tt, h=half, sc=sums[half]:
                            e.scalar_tensor_tensor(acc[:, t * D + h * 512:t * D + h * 512 + 512],
                                                   pb[b][:, :], 1.0, xot[t % 3][:, h * 512:h * 512 + 512],
                                                   ALU.mult, ALU.add, accum_out=sc[:, 0:1]), VE)
        xot_rd[tt % 3] = VE.v
        if tt + 3 < NT:
            W("sync", VE, xot_rd[tt % 3])
            xot_t[(tt + 3) % 3] = dma_s(xot[(tt + 3) % 3][:, :], xo[(tt + 3) * 128:(tt + 4) * 128, :])
        W("vector", VE, wo_ev[(tt * 2) % 6])
        W("vector", VE, wo_ev[(tt * 2 + 1) % 6])
        s1 = OP("vector", lambda e, a=cSum, b_=cS0, c_=cS1: e.tensor_add(a, b_, c_), VE)
        W("vector", VE, s1)
        s2 = OP("vector", lambda e, a=cMean, b_=cSum: e.tensor_scalar_mul(a, b_, 1.0 / D), VE)
        W("vector", VE, s2)
        W("vector", VE, scr_waw[0])
        OP("vector", lambda e, t=tt, m=cMean, o=cNV: e.scalar_tensor_tensor(scr[:, :], acc[:, t * D:t * D + 512],
                                                                            m, acc[:, t * D:t * D + 512],
                                                                            ALU.subtract, ALU.mult, accum_out=o), VE)
        W("vector", VE, VE.v)
        h2 = OP("vector", lambda e, t=tt, m=cMean, o=cH2: e.scalar_tensor_tensor(scr[:, :], acc[:, t * D + 512:(t + 1) * D],
                                                                                 m, acc[:, t * D + 512:(t + 1) * D],
                                                                                 ALU.subtract, ALU.mult, accum_out=o), VE)
        W("vector", VE, h2)
        s3 = OP("vector", lambda e, a=cNV, b_=cH2: e.tensor_add(a, a, b_), VE)
        scr_waw[0] = s3
        W("scalar", VE, s3)
        r1 = OP("scalar", lambda e, a=cRstd, b_=cNV: e.activation(a, b_, AF.Ln, scale=1.0 / D, bias=eps5), SQ)
        W("scalar", SQ, r1)
        r2 = OP("scalar", lambda e, a=cRstd: e.activation(a, a, AF.Exp, scale=-0.5), SQ)
        W("vector", SQ, r2)
        n1 = OP("vector", lambda e, t=tt, m=cMean, r=cRstd: e.tensor_scalar(acc[:, t * D:(t + 1) * D], acc[:, t * D:(t + 1) * D],
                                                                            m, r, ALU.subtract, ALU.mult), VE)
        W("vector", VE, n1)
        W("vector", VE, scr_waw[0])
        OP("vector", lambda e, t=tt, o=cSsq: e.scalar_tensor_tensor(scr[:, :], acc[:, t * D:t * D + 512],
                                                                    0.0, acc[:, t * D:t * D + 512],
                                                                    ALU.subtract, ALU.mult, accum_out=o), VE)
        W("vector", VE, VE.v)
        h2 = OP("vector", lambda e, t=tt, o=cH2: e.scalar_tensor_tensor(scr[:, :], acc[:, t * D + 512:(t + 1) * D],
                                                                        0.0, acc[:, t * D + 512:(t + 1) * D],
                                                                        ALU.subtract, ALU.mult, accum_out=o), VE)
        W("vector", VE, h2)
        s4 = OP("vector", lambda e, a=cSsq, b_=cH2: e.tensor_add(a, a, b_), VE)
        scr_waw[0] = s4
        W("scalar", VE, s4)
        r3 = OP("scalar", lambda e, a=cR2, b_=cSsq: e.activation(a, b_, AF.Ln, scale=1.0 / D, bias=eps6), SQ)
        W("scalar", SQ, r3)
        r4 = OP("scalar", lambda e, a=cR2: e.activation(a, a, AF.Exp, scale=-0.5), SQ)
        W("vector", SQ, r4)
        OP("vector", lambda e, t=tt, a=cR2: e.tensor_copy(r2s[:, t:t + 1], a))
        W("vector", VE, nwbc_t)
        WT("vector", tp_t[tt - 2] if tt >= 2 else None)
        ntok_t[tt % 2] = OP("vector", lambda e, t=tt, r=cR2: e.scalar_tensor_tensor(ntok[t % 2][:, :], acc[:, t * D:(t + 1) * D],
                                                                                    r, nwbc[:, :], ALU.mult, ALU.mult), VE)
        W("sync", VE, ntok_t[tt % 2])
        P.wait("sync", TS[tt % 2], TS[tt % 2].v)
        for dc in range(8):
            tp_t[tt] = dma_fixed("sync", TS[tt % 2], normed[:, dc * TOK + tt * 128:dc * TOK + (tt + 1) * 128],
                                 ntok[tt % 2][:, dc * 128:(dc + 1) * 128], transpose=True)

    y_free = max(wo_wr)
    rt0_tok = dma_s(rtrow[0:1, :], rt_raw[0:1, :])
    for eng in ENGS:
        if eng in ("gpsimd", "sync"):
            queue_drain(eng)
        OP(eng, (lambda e: e.nop()) if eng != "vector" else (lambda e: e.engine_nop()), PH)
    PH2 = PH.v
    for eng in ("sync", "vector", "scalar"):
        W(eng, PH, PH2)
    if debug:
        dy = dma_g(dbg_y[:, :], y_sb[:, :])
        dn = dma_g(dbg_nrm[:, :], normed[:, :])
        da = dma_s(dbg_acc[:, :], acc[:, :])
        dc_ = dma_s(dbg_cmb[:, :], comb[:, :])
        queue_drain("gpsimd")
        queue_drain("sync")
        for eng in ENGS:
            OP(eng, (lambda e: e.nop()) if eng != "vector" else (lambda e: e.engine_nop()), PH)
        PH2b = PH.v
        for eng in ENGS:
            W(eng, PH, PH2b)

    rt_state = {"rt_t": rt0_tok, "rt_rd": 0, "bc_ev": 0, "rtnw_rd": 0, "dots": 0, "comb_done": 0}

    def router_piece(er):
        bws = []
        for half in range(2):
            WT("tensor", rt_state["rt_t"])
            W("tensor", VE, rt_state["bc_ev"])
            W("tensor", VE, wo_ev[5])
            bw = OP("tensor", lambda e, h=half: e.matmul(pb[5][:, :], onesrow[0:1, :],
                    rtrow[0:1, h * 512:(h + 1) * 512], start=True, stop=True), PE)
            bws.append(bw)
            W("vector", PE, bw)
            W("vector", VE, rt_state["rtnw_rd"])
            rt_state["bc_ev"] = OP("vector", lambda e, h=half: e.tensor_mul(rtnw[:, h * 512:(h + 1) * 512],
                                   pb[5][:, :], nwbc[:, h * 512:(h + 1) * 512]), VE)
        rt_state["rt_rd"] = bws[-1]
        if er + 1 < E:
            W("sync", PE, rt_state["rt_rd"])
            rt_state["rt_t"] = dma_s(rtrow[0:1, :], rt_raw[er + 1:er + 2, :])
        for tt in range(NT):
            W("vector", VE, rt_state["bc_ev"])
            W("vector", VE, rt_state["dots"])
            rt_state["dots"] = OP("vector", lambda e, t=tt, e_=er: e.scalar_tensor_tensor(
                dumpf[:, :], acc[:, t * D:(t + 1) * D], 0.0, rtnw[:, :],
                ALU.subtract, ALU.mult, accum_out=lgs[:, t * E + e_:t * E + e_ + 1]), VE)
        rt_state["rtnw_rd"] = rt_state["dots"]

    def router_top2():
        lg = u8[:, 0:E]; uu = u8[:, E:2 * E]; pp = u8[:, 2 * E:3 * E]; eq1 = u8[:, 3 * E:4 * E]
        cand = u8[:, 4 * E:5 * E]; mk1 = u8[:, 5 * E:6 * E]; pp2 = u8[:, 6 * E:7 * E]; mk2 = u8[:, 7 * E:8 * E]
        cM1 = cols[:, 8:9]; cZ = cols[:, 9:10]; cZr = cols[:, 10:11]; cP1 = cols[:, 11:12]
        cFi = cols[:, 12:13]; cM2 = cols[:, 13:14]; cSi = cols[:, 14:15]; cWs = cols[:, 15:16]
        cWr = cols[:, 16:17]; cW1 = cols[:, 17:18]; cW2 = cols[:, 18:19]

        def vop(fn):
            v = OP("vector", fn, VE)
            W("vector", VE, v)
            return v
        for tt in range(NT):
            W("vector", VE, rt_state["dots"])
            WT("vector", io_t)
            W("vector", VE, albc_t)
            vop(lambda e, t=tt: e.tensor_scalar(lg, lgs[:, t * E:(t + 1) * E], r2s[:, t:t + 1], None, ALU.mult))
            vmx = vop(lambda e: e.reduce_max(cM1[:, 0:1], lg, axis=AX.X, negate=True))
            W("scalar", VE, vmx)
            sqe = OP("scalar", lambda e: e.activation(uu, lg, AF.Exp, bias=cM1[:, 0:1]), SQ)
            W("vector", SQ, sqe)
            vop(lambda e: e.reduce_sum(cZ[:, 0:1], uu, axis=AX.X))
            vop(lambda e: e.reciprocal(cZr[:, 0:1], cZ[:, 0:1]))
            vop(lambda e: e.tensor_scalar_mul(pp, uu, cZr[:, 0:1]))
            vop(lambda e: e.reduce_max(cP1[:, 0:1], pp, axis=AX.X))
            vop(lambda e: e.tensor_scalar(eq1, pp, cP1[:, 0:1], None, ALU.is_equal))
            vop(lambda e: e.scalar_tensor_tensor(cand, eq1, -1000.0, r8[:, :], ALU.mult, ALU.add))
            vop(lambda e: e.tensor_reduce(cFi[:, 0:1], cand, axis=AX.X, op=ALU.min))
            vop(lambda e: e.tensor_scalar(mk1, cand, cFi[:, 0:1], None, ALU.is_equal))
            vop(lambda e: e.scalar_tensor_tensor(pp2, mk1, -2.0, pp, ALU.mult, ALU.add))
            vop(lambda e: e.reduce_max(cM2[:, 0:1], pp2, axis=AX.X))
            vop(lambda e: e.tensor_scalar(eq1, pp2, cM2[:, 0:1], None, ALU.is_equal))
            vop(lambda e: e.scalar_tensor_tensor(cand, eq1, -1000.0, r8[:, :], ALU.mult, ALU.add))
            vop(lambda e: e.tensor_reduce(cSi[:, 0:1], cand, axis=AX.X, op=ALU.min))
            vop(lambda e: e.tensor_scalar(mk2, cand, cSi[:, 0:1], None, ALU.is_equal))
            vop(lambda e: e.tensor_add(cWs[:, 0:1], cP1[:, 0:1], cM2[:, 0:1]))
            vop(lambda e: e.reciprocal(cWr[:, 0:1], cWs[:, 0:1]))
            vop(lambda e: e.tensor_mul(cW1[:, 0:1], cP1[:, 0:1], cWr[:, 0:1]))
            vop(lambda e: e.tensor_mul(cW2[:, 0:1], cM2[:, 0:1], cWr[:, 0:1]))
            vop(lambda e: e.tensor_scalar_mul(mk2, mk2, cW2[:, 0:1]))
            vop(lambda e: e.scalar_tensor_tensor(mk1, mk1, cW1[:, 0:1], mk2, ALU.mult, ALU.add))
            rt_state["comb_done"] = vop(lambda e, t=tt: e.tensor_scalar_mul(comb[:, t * E:(t + 1) * E], mk1, smalls[:, 6:7]))

    wgv = wguT.rearrange("e (dc p) n -> e p dc n", p=128)
    wdv = wdT.rearrange("e (hc p) n -> e p hc n", p=128)

    wg_t = {}
    wg_rd = {}
    wd_t = [None] * 8
    wd_rd = 0
    hid_rd = [0, 0]
    hid_wr = [0, 0]
    g_wr = [0, 0]; g_rd = [0, 0]
    u_wr = [0, 0]; u_rd = [0, 0]
    d_wr = [0, 0]; d_ev = [0, 0]
    scr_rd = [0, 0]
    dn_done = [[0] * 2 for _ in range(NT)]
    st_t = [None, None]
    out_rd = [0, 0]

    def load_wg(slot, e_, hc):
        t1 = dma_g(wg[slot * 2][:, :], wgv[e_, :, :, hc * 128:(hc + 1) * 128])
        t2 = dma_g(wg[slot * 2 + 1][:, :], wgv[e_, :, :, HID + hc * 128:HID + (hc + 1) * 128])
        return [t1, t2]

    def load_wd(e_, hc, gi):
        return dma_g(wd[:, (hc - HGRP[gi][0]) * D:(hc - HGRP[gi][0] + 1) * D],
                     wdv[e_, :, hc, :])

    seq = [(e_, gi, hc) for e_ in range(E) for gi in range(3) for hc in range(*HGRP[gi])]
    wg_t[seq[0]] = wg0_t[0]
    wg_t[seq[1]] = wg0_t[1]

    for idx, (e_, gi, hc) in enumerate(seq):
        g0, g1 = HGRP[gi]
        hcl = hc - g0
        hbuf = (e_ * 3 + gi) % 2
        hid = hid0 if hbuf == 0 else hid1
        slot = idx % 2
        W("gpsimd", PE, wd_rd)
        if e_ == 0 and gi == 0 and hcl >= 4:
            W("gpsimd", PE, wg_rd.get(seq[1], 0))
        wd_t[hcl] = load_wd(e_, hc, gi)
        if idx < 2:
            wt_g, wt_u = wg0[hcl * 2], wg0[hcl * 2 + 1]
        else:
            wt_g, wt_u = wg[slot * 2][:, :], wg[slot * 2 + 1][:, :]
        for tq in range(4):
            bg = (hcl * 4 + tq) % 2
            bu = 2 + (hcl * 4 + tq) % 2
            WT("tensor", wg_t[(e_, gi, hc)])
            for k4 in range(4):
                WT("tensor", tp_t[4 * tq + k4])
            W("tensor", SQ, g_rd[bg])
            W("tensor", VE, u_rd[bu - 2])
            W("tensor", VE, wo_ev[bg])
            W("tensor", VE, wo_ev[bu])
            for dc in range(8):
                OP("tensor", lambda e, b=bg, wt=wt_g, dc=dc, tq=tq:
                   e.matmul(pb[b][:, :], wt[:, dc * 128:(dc + 1) * 128],
                            normed[:, dc * TOK + tq * 512:dc * TOK + (tq + 1) * 512],
                            start=(dc == 0), stop=(dc == 7)), PE if dc == 7 else None)
            g_wr[bg] = PE.v
            for dc in range(8):
                OP("tensor", lambda e, b=bu, wt=wt_u, dc=dc, tq=tq:
                   e.matmul(pb[b][:, :], wt[:, dc * 128:(dc + 1) * 128],
                            normed[:, dc * TOK + tq * 512:dc * TOK + (tq + 1) * 512],
                            start=(dc == 0), stop=(dc == 7)), PE if dc == 7 else None)
            u_wr[bu - 2] = PE.v
            sh = (hcl * 4 + tq) % 2
            W("scalar", PE, g_wr[bg])
            W("scalar", VE, scr_rd[sh])
            sl = OP("scalar", lambda e, b=bg, sh=sh: e.activation(sil[sh][:, :], pb[b][:, :], AF.Silu), SQ)
            g_rd[bg] = sl
            W("vector", SQ, sl)
            W("vector", PE, u_wr[bu - 2])
            W("vector", PE, hid_rd[hbuf])
            W("vector", PE, y_free)
            mu = OP("vector", lambda e, b=bu, sh=sh, hid=hid, hcl=hcl, tq=tq:
                    e.tensor_mul(hid[:, hcl * TOK + tq * 512:hcl * TOK + (tq + 1) * 512],
                                 sil[sh][:, :], pb[b][:, :]), VE)
            u_rd[bu - 2] = mu
            scr_rd[sh] = mu
        if e_ == 0 and gi == 0:
            router_piece(hcl)
            if hcl == 7:
                router_top2()
                d_ev[1] = max(d_ev[1], rt_state["bc_ev"])
        if idx == 0:
            d_ev[0] = max(d_ev[0], wo_ev[4])
            d_ev[1] = max(d_ev[1], wo_ev[5])
        hid_wr[hbuf] = VE.v
        if hc == g1 - 1:
            nhc = g1 - g0
            W("tensor", VE, hid_wr[hbuf])
            for hcx in range(nhc):
                WT("tensor", wd_t[hcx])
            for tt in range(NT):
                for half in range(2):
                    bd = 4 + (tt * 2 + half) % 2
                    W("tensor", VE, d_ev[(tt * 2 + half) % 2])
                    for hcx in range(nhc):
                        OP("tensor", lambda e, b=bd, hid=hid, hcx=hcx, tt=tt, hf=half, n_=nhc:
                           e.matmul(pb[b][:, :], hid[:, hcx * TOK + tt * 128:hcx * TOK + (tt + 1) * 128],
                                    wd[:, hcx * D + hf * 512:hcx * D + hf * 512 + 512],
                                    start=(hcx == 0), stop=(hcx == n_ - 1)), PE if hcx == nhc - 1 else None)
                    d_wr[(tt * 2 + half) % 2] = PE.v
                    W("vector", PE, d_wr[(tt * 2 + half) % 2])
                    W("vector", VE, rt_state["comb_done"])
                    d_ev[(tt * 2 + half) % 2] = OP(
                        "vector", lambda e, b=bd, tt=tt, hf=half, e_=e_:
                        e.scalar_tensor_tensor(acc[:, tt * D + hf * 512:tt * D + hf * 512 + 512],
                                               pb[b][:, :], comb[:, tt * E + e_:tt * E + e_ + 1],
                                               acc[:, tt * D + hf * 512:tt * D + hf * 512 + 512],
                                               ALU.mult, ALU.add), VE)
                    dn_done[tt][half] = d_ev[(tt * 2 + half) % 2]
                if e_ == E - 1 and gi == 2:
                    ob = tt % 2
                    W("vector", VE, dn_done[tt][1])
                    WT("vector", st_t[ob])
                    f1 = OP("vector", lambda e, t=tt: e.tensor_reduce(cols[:, 20:21], acc[:, t * D:(t + 1) * D], axis=AX.X, op=ALU.add), VE)
                    W("vector", VE, f1)
                    f2 = OP("vector", lambda e: e.tensor_scalar_mul(cols[:, 21:22], cols[:, 20:21], 1.0 / D), VE)
                    W("vector", VE, f2)
                    W("vector", VE, scr_waw[0])
                    OP("vector", lambda e, t=tt: e.scalar_tensor_tensor(scr[:, :], acc[:, t * D:t * D + 512],
                                                                        cols[:, 21:22], acc[:, t * D:t * D + 512],
                                                                        ALU.subtract, ALU.mult, accum_out=cols[:, 22:23]), VE)
                    W("vector", VE, VE.v)
                    h2 = OP("vector", lambda e, t=tt: e.scalar_tensor_tensor(scr[:, :], acc[:, t * D + 512:(t + 1) * D],
                                                                             cols[:, 21:22], acc[:, t * D + 512:(t + 1) * D],
                                                                             ALU.subtract, ALU.mult, accum_out=cols[:, 24:25]), VE)
                    W("vector", VE, h2)
                    f3 = OP("vector", lambda e: e.tensor_add(cols[:, 22:23], cols[:, 22:23], cols[:, 24:25]), VE)
                    W("vector", VE, f3)
                    scr_waw[0] = f3
                    W("scalar", VE, f3)
                    f4 = OP("scalar", lambda e: e.activation(cols[:, 23:24], cols[:, 22:23], AF.Ln, scale=1.0 / D, bias=smalls[:, 8:9]), SQ)
                    W("scalar", SQ, f4)
                    f5 = OP("scalar", lambda e: e.activation(cols[:, 23:24], cols[:, 23:24], AF.Exp, scale=-0.5), SQ)
                    W("vector", SQ, f5)
                    f6 = OP("vector", lambda e, t=tt, ob=ob: e.tensor_scalar(outsb[ob][:, :], acc[:, t * D:(t + 1) * D],
                                                                             cols[:, 21:22], cols[:, 23:24],
                                                                             ALU.subtract, ALU.mult), VE)
                    W("sync", VE, f6)
                    P.wait("sync", STS[tt % 2], STS[tt % 2].v)
                    st_t[ob] = dma_fixed("sync", STS[tt % 2], out[tt * 128:(tt + 1) * 128, :], outsb[ob][:, :])
            wd_rd = PE.v
            hid_rd[hbuf] = PE.v
        wg_rd[(e_, gi, hc)] = PE.v
        if idx + 2 < len(seq):
            nxt = seq[idx + 2]
            W("gpsimd", PE, wg_rd[(e_, gi, hc)])
            W("gpsimd", VE, albc_t)
            wg_t[nxt] = load_wg(slot, nxt[0], nxt[2])

    queue_drain("sync")

    with nc.Block() as block:
        @block.sync
        def _(e):
            for f in P.ops["sync"]:
                f(e)

        @block.gpsimd
        def _(e):
            for f in P.ops["gpsimd"]:
                f(e)

        @block.vector
        def _(e):
            for f in P.ops["vector"]:
                f(e)

        @block.scalar
        def _(e):
            for f in P.ops["scalar"]:
                f(e)

        @block.tensor
        def _(e):
            for f in P.ops["tensor"]:
                f(e)

    return nc


def kernel(L0_attn_x, L0_attn_W_Q, L0_attn_W_K, L0_attn_W_V, L0_attn_W_O,
           L0_ffn_norm_weight, L0_ffn_router_weight, L0_ffn_gate_up_weight,
           L0_ffn_down_weight, L0_ffn_residual_alpha):
    x = np.ascontiguousarray(np.asarray(L0_attn_x, dtype=np.float32))
    wqT = np.ascontiguousarray(np.asarray(L0_attn_W_Q, np.float32).T)
    wkT = np.ascontiguousarray(np.asarray(L0_attn_W_K, np.float32).T)
    wvT = np.ascontiguousarray(np.asarray(L0_attn_W_V, np.float32).T)
    woT = np.ascontiguousarray(np.asarray(L0_attn_W_O, np.float32).T)
    rt_raw = np.ascontiguousarray(np.asarray(L0_ffn_router_weight, np.float32))
    nw = np.asarray(L0_ffn_norm_weight, np.float32).reshape(1, D)
    al = np.asarray(L0_ffn_residual_alpha, np.float32).reshape(1, 1)
    wguT = np.ascontiguousarray(np.asarray(L0_ffn_gate_up_weight, np.float32).transpose(0, 2, 1))
    wdT = np.ascontiguousarray(np.asarray(L0_ffn_down_weight, np.float32).transpose(0, 2, 1))
    iota = np.broadcast_to(np.arange(E, dtype=np.float32), (128, E)).copy()

    tk = np.arange(128)[:, None]
    tq = np.arange(512)[None, :]
    diag = [(tq >= tk + r * 128).astype(np.float32) for r in range(4)]
    ones = np.ones((128, 512), np.float32)
    zeros = np.zeros((128, 512), np.float32)
    maskA = np.concatenate(diag + [zeros] * 4, axis=0)
    maskB = np.concatenate([ones] * 4 + diag, axis=0)

    xTb = [np.ascontiguousarray(x[b].T) for b in range(B)]
    nc = build_nc()
    in_maps = []
    slots = []
    for c in range(8):
        b, half = c // 2, c % 2
        qbs = [(2 * i + half) * 512 for i in range(4)]
        slots.append((b, qbs))
        xo = np.ascontiguousarray(np.concatenate([x[b, qb:qb + 512] for qb in qbs], axis=0))
        xqv = np.ascontiguousarray(xo.T)
        in_maps.append({
            "xT": xTb[b], "xq": xqv, "xo": xo,
            "wqT": wqT, "wkT": wkT, "wvT": wvT, "woT": woT,
            "rt_raw": rt_raw, "nw": nw, "alin": al,
            "wguT": wguT, "wdT": wdT,
            "masks": maskA if half == 0 else maskB,
            "iota": iota,
        })
    res = run_bass_kernel_spmd(nc, in_maps, core_ids=list(range(8)))
    outp = np.empty((B, S, D), np.float32)
    for c in range(8):
        b, qbs = slots[c]
        oc = res.results[c]["out"]
        for i, qb in enumerate(qbs):
            outp[b, qb:qb + 512] = oc[i * 512:(i + 1) * 512]
    return outp
